# version 08
# baseline (speedup 1.0000x reference)
import numpy as np
import ml_dtypes
import concourse.bass as bass
import concourse.mybir as mybir
from concourse.bass_utils import run_bass_kernel_spmd
from contextlib import ExitStack

F32 = mybir.dt.float32
BF16 = mybir.dt.bfloat16
AF = mybir.ActivationFunctionType
ALU = mybir.AluOpType
AX = mybir.AxisListType

P = 128
D = 1024
KT = 8
NT = 16
NTOK = 2048
NE = 8
DFF = 4096
NF = 32
H = 16
HD = 64
NGT = 32
SLOTS = 4
EPS = 1e-5

CFG = {"NE": 8, "DO_MOE1": True, "DO_ATT": True, "DO_MOE3": True}

ENGS = ("sync", "gpsimd", "tensor", "vector", "scalar")


class Dyn:

    def __init__(self, fn):
        self.fn = fn


_QOFF_CACHE = {}


def qoff_expr(e, s_):
    key = (id(e), s_)
    if key not in _QOFF_CACHE:
        pk = (id(e), "pid")
        if pk not in _QOFF_CACHE:
            _QOFF_CACHE[pk] = e.partition_id() % 2
        _QOFF_CACHE[key] = _QOFF_CACHE[pk] * 512 + s_ * 1024
    return _QOFF_CACHE[key]


class Prog:
    def __init__(self):
        self.q = {e: [] for e in ENGS}
        self.cnt = {}

    def sem(self, name):
        self.cnt[name] = 0

    def op(self, eng, method, args=(), kwargs=None, waits=(), inc=None, incby=1):
        waits = tuple((s, v) for (s, v) in waits if v)
        self.q[eng].append((method, tuple(args), dict(kwargs or {}), waits, inc,
                            incby))
        if inc is not None:
            self.cnt[inc] += incby
            return self.cnt[inc]
        return None

    def emit(self, handle, eng, sems):
        for method, args, kwargs, waits, inc, incby in self.q[eng]:
            for s, v in waits:
                handle.wait_ge(sems[s], v)
            a = [x.fn(handle) if isinstance(x, Dyn) else x for x in args]
            kw = {k: (x.fn(handle) if isinstance(x, Dyn) else x)
                  for k, x in kwargs.items()}
            inst = getattr(handle, method)(*a, **kw)
            if inc is not None:
                inst.then_inc(sems[inc], incby)


def build_nc():
    nc = bass.Bass("TRN2", target_bir_lowering=False)
    ne = CFG["NE"]

    x_in = nc.dram_tensor("x_in", [NTOK, D], F32, kind="ExternalInput")
    rw_h = nc.dram_tensor("rw", [NE, D], F32, kind="ExternalInput")
    wg_h = nc.dram_tensor("wg", [NE, D, DFF], F32, kind="ExternalInput")
    wu_h = nc.dram_tensor("wu", [NE, D, DFF], F32, kind="ExternalInput")
    wd_h = nc.dram_tensor("wd", [NE, DFF, D], F32, kind="ExternalInput")
    masks_h = nc.dram_tensor("masks", [P, 8, 512], BF16, kind="ExternalInput")
    id_h = nc.dram_tensor("ident32", [P, P], F32, kind="ExternalInput")
    tril_h = nc.dram_tensor("trilput", [P, P], BF16, kind="ExternalInput")
    iota_h = nc.dram_tensor("iota384", [P, 384], mybir.dt.float16, kind="ExternalInput")
    x_out = nc.dram_tensor("x_outp", [NTOK, D], F32, kind="ExternalOutput")
    cc_in = nc.dram_tensor("cc_in", [NTOK, D], BF16, kind="Internal")
    cc_out = nc.dram_tensor("cc_out", [2 * NTOK, D], BF16, kind="Internal")

    x_in3 = x_in.ap().rearrange("(t p) d -> p t d", p=P)
    x_out3 = x_out.ap().rearrange("(t p) d -> p t d", p=P)
    cc_in3 = cc_in.ap().rearrange("(t p) d -> p t d", p=P)

    pr = Prog()
    SEMS = ("ldx", "ldi", "ldr", "ldm", "ldy", "ldc0", "ldc1", "ldt", "ldo",
            "t", "v", "s", "cc", "st") + tuple(f"gd{i}" for i in range(6))
    for s in SEMS:
        pr.sem(s)
    C = pr.cnt

    es = ExitStack()
    with es:
        sems = {s: es.enter_context(nc.semaphore("sem_" + s)) for s in SEMS}

        xs = es.enter_context(nc.sbuf_tensor("xs", [P, NT, D], F32))
        id32 = es.enter_context(nc.sbuf_tensor("id32", [P, P], F32))
        idbf = es.enter_context(nc.sbuf_tensor("idbf", [P, P], BF16))
        rwTh = es.enter_context(nc.sbuf_tensor("rwTh", [P, KT, NE], BF16))
        rwTr = es.enter_context(nc.sbuf_tensor("rwTr", [P, KT, NE], BF16))
        cwb = es.enter_context(nc.sbuf_tensor("cwb", [P, NT, NE], F32))
        sc = es.enter_context(nc.sbuf_tensor("scs", [P, NT, 12], F32))
        onecol = es.enter_context(nc.sbuf_tensor("onecol", [P, 1], BF16))
        epscol = es.enter_context(nc.sbuf_tensor("epscol", [P, 1], F32))
        estmp = es.enter_context(nc.sbuf_tensor("estmp", [P, 2, NE], F32))
        estmp2 = es.enter_context(nc.sbuf_tensor("estmp2", [P, 2, NE], F32))
        trilput = es.enter_context(nc.sbuf_tensor("trilput_s", [P, P], BF16))
        iota384 = es.enter_context(nc.sbuf_tensor("iota384_s", [P, 384], mybir.dt.float16))

        pr.op("sync", "dma_start", kwargs=dict(out=xs[:, :, :], in_=x_in3),
              inc="ldx", incby=16)
        pr.op("sync", "dma_start", kwargs=dict(out=id32[:, :], in_=id_h[:, :]),
              inc="ldi", incby=16)
        v_idbf = pr.op("vector", "tensor_copy", (idbf[:, :], id32[:, :]),
                       waits=[("ldi", 16)], inc="v")
        v_onec = pr.op("vector", "memset", (onecol[:, :], 1.0), inc="v")
        v_eps = pr.op("vector", "memset", (epscol[:, :], EPS), inc="v")
        pr.op("sync", "dma_start", kwargs=dict(out=trilput[:, :],
                                               in_=tril_h[:, :]),
              inc="ldt", incby=16)
        pr.op("sync", "dma_start", kwargs=dict(out=iota384[:, :],
                                               in_=iota_h[:, :]),
              inc="ldo", incby=16)

        with ExitStack() as ies:
            rws = ies.enter_context(nc.sbuf_tensor("rws", [NE, D], F32))
            rwT32 = ies.enter_context(nc.sbuf_tensor("rwT32", [P, KT, NE], F32))
            trp32 = ies.enter_context(nc.psum_tensor("trp32", [P, NE], F32))
            pr.op("sync", "dma_start", kwargs=dict(out=rws[:, :], in_=rw_h[:, :]),
                  inc="ldr", incby=16)
            prev_cp = 0
            for k in range(KT):
                tc_ = pr.op("tensor", "transpose",
                            (trp32[:, :], rws[0:NE, k * P:(k + 1) * P],
                             id32[0:NE, 0:NE]),
                            waits=[("ldr", 16), ("ldi", 16), ("v", prev_cp)],
                            inc="t")
                prev_cp = pr.op("vector", "tensor_copy",
                                (rwT32[:, k, :], trp32[:, :]),
                                waits=[("t", tc_)], inc="v")
            vh = pr.op("vector", "tensor_copy", (rwTh[:, :, :], rwT32[:, :, :]),
                       waits=[("v", prev_cp)], inc="v")
            v_rw_done = pr.op("vector", "tensor_tensor",
                              (rwTr[:, :, :], rwT32[:, :, :], rwTh[:, :, :],
                               ALU.subtract),
                              waits=[("v", vh)], inc="v")

        def frame_barrier():
            snap = [(s, C[s]) for s in SEMS if C[s] > 0]
            for eng in ENGS:
                pr.op(eng, "nop", waits=snap)

        def ln_stats_a(t, dump, x_ready_v, prev_ssq=0):
            s_ = sc[:, t, :]
            vsum = pr.op("vector", "reduce_sum", (s_[:, 0:1], xs[:, t, :]),
                         dict(axis=AX.X),
                         waits=[("ldx", 16), ("v", x_ready_v)], inc="v")
            ssq = pr.op("scalar", "activation", (dump[:, :], xs[:, t, :],
                                                 AF.Square),
                        dict(accum_out=s_[:, 1:2]),
                        waits=[("ldx", 16), ("v", x_ready_v),
                               ("s", prev_ssq)], inc="s")
            return vsum, ssq

        def ln_stats_b(t, vsum, ssq):
            s_ = sc[:, t, :]
            v1 = pr.op("vector", "tensor_scalar",
                       (s_[:, 2:3], s_[:, 0:1], -1.0 / D, None, ALU.mult),
                       waits=[("v", vsum)], inc="v")
            v2 = pr.op("vector", "tensor_tensor",
                       (s_[:, 3:4], s_[:, 2:3], s_[:, 2:3], ALU.mult),
                       waits=[("v", v1)], inc="v")
            v3 = pr.op("vector", "tensor_scalar",
                       (s_[:, 4:5], s_[:, 1:2], 1.0 / D, None, ALU.mult),
                       waits=[("s", ssq), ("v", v2)], inc="v")
            v4 = pr.op("vector", "tensor_tensor",
                       (s_[:, 5:6], s_[:, 4:5], s_[:, 3:4], ALU.subtract),
                       waits=[("v", v3)], inc="v")
            s1 = pr.op("scalar", "activation", (s_[:, 6:7], s_[:, 5:6], AF.Sqrt),
                       dict(bias=epscol[:, :]),
                       waits=[("v", v4), ("v", v_eps)], inc="s")
            v5 = pr.op("vector", "reciprocal", (s_[:, 7:8], s_[:, 6:7]),
                       waits=[("s", s1)], inc="v")
            v6 = pr.op("vector", "tensor_tensor",
                       (s_[:, 3:4], s_[:, 2:3], s_[:, 7:8], ALU.mult),
                       waits=[("v", v5)], inc="v")
            return v6

        def ln_norm(t, stats_v, y32, yb, yres, extra_v7=(), extra_v8=(),
                    extra_v9=()):
            s_ = sc[:, t, :]
            v7 = pr.op("vector", "tensor_scalar",
                       (y32[:, :], xs[:, t, :], s_[:, 7:8], s_[:, 3:4],
                        ALU.mult, ALU.add),
                       waits=[("v", stats_v)] + list(extra_v7), inc="v")
            v8 = pr.op("vector", "tensor_copy", (yb[:, :], y32[:, :]),
                       waits=[("v", v7)] + list(extra_v8), inc="v")
            v9 = v8
            if yres is not None:
                v9 = pr.op("vector", "tensor_tensor",
                           (yres[:, :], y32[:, :], yb[:, :], ALU.subtract),
                           waits=[("v", v8)] + list(extra_v9), inc="v")
            return v8, v9

        def moe_layer(tag):
            frame_barrier()
            x_ready_v = C["v"]
            CAP = 384
            NST = CAP // P
            fes = ExitStack()
            with fes:
                yst = fes.enter_context(nc.sbuf_tensor(f"yst_{tag}", [P, D], F32))
                yrs = fes.enter_context(nc.sbuf_tensor(f"yrs_{tag}", [P, D], BF16))
                ytokm = fes.enter_context(
                    nc.sbuf_tensor(f"ytokm_{tag}", [P, 8, D], BF16))
                ytg = fes.enter_context(
                    nc.sbuf_tensor(f"ytg_{tag}", [P, 2, KT, P], BF16))
                yrtg = fes.enter_context(
                    nc.sbuf_tensor(f"yrtg_{tag}", [P, KT, P], BF16))
                ht = fes.enter_context(
                    nc.sbuf_tensor(f"ht_{tag}", [P, NF, CAP], BF16))
                warena = fes.enter_context(
                    nc.sbuf_tensor(f"wa_{tag}", [P, 48, 512], BF16))
                sg = fes.enter_context(nc.sbuf_tensor(f"sg_{tag}", [P, 2, CAP],
                                                      BF16))
                xet = fes.enter_context(
                    nc.sbuf_tensor(f"xet_{tag}", [P, KT, CAP], BF16))
                hds = fes.enter_context(
                    nc.sbuf_tensor(f"hds_{tag}", [P, NST, 512], BF16))
                pgb = [fes.enter_context(
                    nc.sbuf_tensor(f"pg{i}_{tag}", [P, 8, CAP], BF16))
                    for i in range(2)]
                psb = [fes.enter_context(
                    nc.sbuf_tensor(f"ps{i}_{tag}", [P, 8 * NST, P], BF16))
                    for i in range(2)]
                scr = fes.enter_context(nc.sbuf_tensor(f"scr_{tag}", [P, 2, 48],
                                                       F32))
                rcbf = fes.enter_context(nc.sbuf_tensor(f"rcbf_{tag}", [P, 2, 1],
                                                        BF16))

                slot_busy_t = [0] * 6
                slot_ld = {}
                dma_i = [0]

                def load_chunk(key, src_ap):
                    if key in slot_ld:
                        return
                    slot = dma_i[0] % 6
                    dma_i[0] += 1
                    gdc = pr.op("gpsimd", "dma_start",
                                kwargs=dict(
                                    out=warena[:, slot * 8:(slot + 1) * 8, :],
                                    in_=src_ap),
                                waits=[("t", slot_busy_t[slot])],
                                inc=f"gd{slot}", incby=16)
                    slot_ld[key] = (slot, gdc)

                def wg_ap(e_, c):
                    return wg_h.ap()[e_, :, c * 512:(c + 1) * 512].rearrange(
                        "(k p) n -> p k n", p=P)

                def wu_ap(e_, c):
                    return wu_h.ap()[e_, :, c * 512:(c + 1) * 512].rearrange(
                        "(k p) n -> p k n", p=P)

                def wd_ap(e_, dc, c):
                    return wd_h.ap()[e_, c * 1024:(c + 1) * 1024,
                                     dc * 512:(dc + 1) * 512].rearrange(
                        "(k p) n -> p k n", p=P)

                ytokm_rd_t = 0
                yrs_rd_t = 0
                prev_evac_all = 0
                stats_v = {}
                ab = {}
                pssq = 0
                for t in range(NT):
                    ab[t] = ln_stats_a(t, yst, x_ready_v, pssq)
                    pssq = ab[t][1]
                for t in range(NT):
                    stats_v[t] = ln_stats_b(t, *ab[t])
                for tg in range(2):
                    tiles = list(range(tg * 8, tg * 8 + 8))
                    ytg_free_t = C["t"]
                    with ExitStack() as pes:
                        trps = [pes.enter_context(
                            nc.psum_tensor(f"trp_{tag}{tg}{i}", [P, 4, P],
                                           BF16)) for i in range(2)]
                        lps = pes.enter_context(
                            nc.psum_tensor(f"lp_{tag}{tg}", [P, NE], F32))
                        vbar = C["v"]
                        tr_busy = [vbar, vbar]
                        tri = [0]
                        prev_lps_rd = 0
                        yrtg_rd_t = 0
                        ytgb_rd_t = [ytg_free_t, ytg_free_t]
                        prev_es_rd = [0, 0]
                        prev_es2_rd = [0, 0]

                        def tr16(src, dst4s, vready, extra_copy_w):
                            last = 0
                            for g4 in range(2):
                                bank = tri[0] % 2
                                tri[0] += 1
                                tc_ = 0
                                for j in range(4):
                                    jj = g4 * 4 + j
                                    w = [("v", vready), ("v", v_idbf),
                                         ("t", ytg_free_t)]
                                    if j == 0:
                                        w.append(("v", tr_busy[bank]))
                                    tc_ = pr.op(
                                        "tensor", "transpose",
                                        (trps[bank][:, j, :],
                                         src[:, jj * P:(jj + 1) * P],
                                         idbf[:, :]),
                                        waits=w, inc="t")
                                vc = pr.op("vector", "tensor_copy",
                                           (dst4s[g4], trps[bank][:, :, :]),
                                           waits=[("t", tc_)] + extra_copy_w,
                                           inc="v")
                                tr_busy[bank] = vc
                                last = vc
                            return last

                        for t in tiles:
                            tt = t - tg * 8
                            tb = t % 2
                            v8, v9 = ln_norm(
                                t, stats_v[t], yst[:, :], ytokm[:, tt, :],
                                yrs[:, :],
                                extra_v8=[("t", ytokm_rd_t)],
                                extra_v9=[("t", yrs_rd_t)])
                            c1 = tr16(ytokm[:, tt, :],
                                      [ytg[:, tb, 0:4, :], ytg[:, tb, 4:8, :]],
                                      v8, [("t", ytgb_rd_t[tb])])
                            c2 = tr16(yrs[:, :],
                                      [yrtg[:, 0:4, :], yrtg[:, 4:8, :]],
                                      v9, [("t", yrtg_rd_t)])
                            yrs_rd_t = C["t"]
                            vready = max(c1, c2)
                            w0 = [("v", vready), ("v", v_rw_done),
                                  ("s", prev_lps_rd)]
                            last = 0
                            for ph in range(3):
                                rhs_t = rwTh if ph < 2 else rwTr
                                for k in range(KT):
                                    first = (ph == 0 and k == 0)
                                    stop = (ph == 2 and k == KT - 1)
                                    lhs = (yrtg[:, k, :] if ph == 1
                                           else ytg[:, tb, k, :])
                                    last = pr.op(
                                        "tensor", "matmul",
                                        (lps[:, :], lhs, rhs_t[:, k, :]),
                                        dict(start=first, stop=stop),
                                        waits=(w0 if first else []), inc="t")
                            yrtg_rd_t = last
                            ytgb_rd_t[tb] = last
                            s_ = sc[:, t, 8:12]
                            eb = t % 2
                            esl = estmp[:, eb, :]
                            es2 = estmp2[:, eb, :]
                            cv1 = pr.op("vector", "reduce_max",
                                        (s_[:, 0:1], lps[:, :]),
                                        dict(axis=AX.X, negate=True),
                                        waits=[("t", last)], inc="v")
                            ces = pr.op("scalar", "activation",
                                        (esl, lps[:, :], AF.Exp),
                                        dict(bias=s_[:, 0:1]),
                                        waits=[("v", cv1),
                                               ("v", prev_es_rd[eb])], inc="s")
                            prev_lps_rd = ces
                            cv2 = pr.op("vector", "tensor_scalar",
                                        (es2, esl, 1.0, None, ALU.is_lt),
                                        waits=[("s", ces),
                                               ("v", prev_es2_rd[eb])], inc="v")
                            cv3 = pr.op("vector", "tensor_tensor",
                                        (es2, esl, es2, ALU.mult),
                                        waits=[("v", cv2)], inc="v")
                            cv4 = pr.op("vector", "reduce_max",
                                        (s_[:, 1:2], es2), dict(axis=AX.X),
                                        waits=[("v", cv3)], inc="v")
                            cv5 = pr.op("vector", "tensor_scalar",
                                        (es2, esl, s_[:, 1:2], None, ALU.is_ge),
                                        waits=[("v", cv4)], inc="v")
                            cv6 = pr.op("vector", "tensor_tensor",
                                        (es2, esl, es2, ALU.mult),
                                        waits=[("v", cv5)], inc="v")
                            prev_es_rd[eb] = cv6
                            cv7 = pr.op("vector", "tensor_scalar",
                                        (s_[:, 2:3], s_[:, 1:2], 1.0, None,
                                         ALU.add),
                                        waits=[("v", cv6)], inc="v")
                            cv8 = pr.op("vector", "reciprocal",
                                        (s_[:, 3:4], s_[:, 2:3]),
                                        waits=[("v", cv7)], inc="v")
                            cv9 = pr.op("vector", "tensor_scalar",
                                        (cwb[:, t, :], es2, s_[:, 3:4], 0.5,
                                         ALU.mult, ALU.mult),
                                        waits=[("v", cv8)], inc="v")
                            prev_es2_rd[eb] = cv9
                    cw_done_v = C["v"]
                    ytokm_done_v = C["v"]

                    pg_rd_t = [0, 0]
                    ps_rd_t = [0, 0]
                    prev_hds_rd = 0
                    with ExitStack() as bfr:
                        bldcp = bfr.enter_context(
                            nc.psum_tensor(f"bcp_{tag}{tg}", [P, 1], F32))
                        bld_prev_bv2 = [0]
                        bstate = {}

                        def emit_build_dve(e_):
                            eb2 = e_ % 2
                            pg = pgb[eb2]
                            s2 = scr[:, eb2, :]
                            bv0 = pr.op("vector", "tensor_scalar",
                                        (s2[:, 0:8],
                                         cwb[:, tg * 8:(tg + 1) * 8, e_],
                                         0.0, None, ALU.is_gt),
                                        waits=[("v", cw_done_v),
                                               ("t", pg_rd_t[eb2])], inc="v")
                            bv1 = pr.op("vector", "reduce_sum",
                                        (s2[:, 8:9], s2[:, 0:8]),
                                        dict(axis=AX.X),
                                        waits=[("v", bv0)], inc="v")
                            bv1b = pr.op("vector", "tensor_copy",
                                         (rcbf[:, eb2, :], s2[:, 8:9]),
                                         waits=[("v", bv1)], inc="v")
                            bt1 = pr.op("tensor", "matmul",
                                        (bldcp[:, :], trilput[:, :],
                                         rcbf[:, eb2, :]),
                                        dict(start=True, stop=True),
                                        waits=[("v", bv1b), ("ldt", 16),
                                               ("v", bld_prev_bv2[0])],
                                        inc="t")
                            bv2 = pr.op("vector", "tensor_copy",
                                        (s2[:, 9:10], bldcp[:, :]),
                                        waits=[("t", bt1)], inc="v")
                            bld_prev_bv2[0] = bv2
                            a_, b_ = 16, 24
                            pr.op("vector", "tensor_copy",
                                  (s2[:, a_:a_ + 8], s2[:, 0:8]),
                                  waits=[("v", bv0)], inc="v")
                            cur, oth = a_, b_
                            lastp = C["v"]
                            for kshift in (1, 2, 4):
                                p1 = pr.op("vector", "tensor_tensor",
                                           (s2[:, oth + kshift:oth + 8],
                                            s2[:, cur + kshift:cur + 8],
                                            s2[:, cur:cur + 8 - kshift],
                                            ALU.add),
                                           waits=[("v", lastp)], inc="v")
                                p2 = pr.op("vector", "tensor_copy",
                                           (s2[:, oth:oth + kshift],
                                            s2[:, cur:cur + kshift]),
                                           waits=[("v", lastp)], inc="v")
                                lastp = max(p1, p2)
                                cur, oth = oth, cur
                            bv3 = pr.op("vector", "tensor_tensor",
                                        (s2[:, 32:40], s2[:, cur:cur + 8],
                                         s2[:, 0:8], ALU.subtract),
                                        waits=[("v", lastp)], inc="v")
                            bv4 = pr.op("vector", "tensor_scalar",
                                        (s2[:, 32:40], s2[:, 32:40],
                                         s2[:, 9:10], None, ALU.add),
                                        waits=[("v", bv3), ("v", bv2)],
                                        inc="v")
                            pg_done = 0
                            for tt in range(8):
                                pg_done = pr.op(
                                    "vector", "tensor_scalar",
                                    (pg[:, tt, :], iota384[:, :],
                                     s2[:, 32 + tt:33 + tt],
                                     s2[:, tt:tt + 1],
                                     ALU.is_equal, ALU.mult),
                                    waits=[("v", bv4), ("ldo", 16)], inc="v")
                            return pg_done

                        def emit_ps_transposes(e_, pg_done, bank_guard_v):
                            eb2 = e_ % 2
                            pg = pgb[eb2]
                            psx = psb[eb2]
                            ps_done = 0
                            with ExitStack() as tfr:
                                ptr4 = [tfr.enter_context(
                                    nc.psum_tensor(f"ptr_{tag}{tg}{e_}{i}",
                                                   [P, 4, P], BF16))
                                    for i in range(2)]
                                pb4 = [0, 0]
                                for g4 in range(6):
                                    bank = g4 % 2
                                    tpc = 0
                                    for j in range(4):
                                        fl = g4 * 4 + j
                                        tt, st = fl // NST, fl % NST
                                        w = [("v", pg_done), ("v", v_idbf),
                                             ("v", bank_guard_v)]
                                        if j == 0:
                                            w.append(("v", pb4[bank]))
                                        if fl == 0:
                                            w.append(("t", ps_rd_t[eb2]))
                                        tpc = pr.op(
                                            "tensor", "transpose",
                                            (ptr4[bank][:, j, :],
                                             pg[:, tt, st * P:(st + 1) * P],
                                             idbf[:, :]),
                                            waits=w, inc="t")
                                    ps_done = pr.op(
                                        "vector", "tensor_copy",
                                        (psx[:, g4 * 4:(g4 + 1) * 4, :],
                                         ptr4[bank][:, :, :]),
                                        waits=[("t", tpc)], inc="v")
                                    pb4[bank] = ps_done
                            return ps_done

                        def emit_gather(e_, pg_done, ps_done):
                            eb2 = e_ % 2
                            pg = pgb[eb2]
                            with ExitStack() as ges:
                                gtp = [ges.enter_context(
                                    nc.psum_tensor(f"gt_{tag}{tg}{e_}{i}",
                                                   [P, CAP], F32))
                                    for i in range(2)]
                                gbusy = [0, 0]
                                xet_done = 0
                                for k8 in range(KT):
                                    bank = k8 % 2
                                    gl = 0
                                    for tt in range(8):
                                        w = []
                                        if tt == 0:
                                            w = [("v", pg_done),
                                                 ("v", ps_done),
                                                 ("v", ytokm_done_v),
                                                 ("v", gbusy[bank])]
                                        gl = pr.op(
                                            "tensor", "matmul",
                                            (gtp[bank][:, :],
                                             ytokm[:, tt, k8 * P:(k8 + 1) * P],
                                             pg[:, tt, :]),
                                            dict(start=(tt == 0),
                                                 stop=(tt == 7)),
                                            waits=w, inc="t")
                                    xet_done = pr.op(
                                        "vector", "tensor_copy",
                                        (xet[:, k8, :], gtp[bank][:, :]),
                                        waits=[("t", gl)], inc="v")
                                    gbusy[bank] = xet_done
                                pg_rd_t[eb2] = C["t"]
                            return xet_done

                        def emit_gu(e_, xet_done, prev_evac_all):
                            with ExitStack() as ges2:
                                gp = ges2.enter_context(
                                    nc.psum_tensor(f"gp_{tag}{tg}{e_}",
                                                   [P, 2, 512], F32))
                                up = ges2.enter_context(
                                    nc.psum_tensor(f"up_{tag}{tg}{e_}",
                                                   [P, 2, 512], F32))
                                silu_hist = [0, 0]
                                h_hist = [0, 0]
                                prev_silu_s = 0
                                prev_h_v = 0
                                for f in range(NF):
                                    c = f // 4
                                    if f % 4 == 0:
                                        if c == 0:
                                            for cc0 in (0, 1):
                                                load_chunk(("wg", e_, tg, cc0),
                                                           wg_ap(e_, cc0))
                                                load_chunk(("wu", e_, tg, cc0),
                                                           wu_ap(e_, cc0))
                                        elif c < 5:
                                            load_chunk(("wg", e_, tg, c + 1),
                                                       wg_ap(e_, c + 1))
                                            load_chunk(("wu", e_, tg, c + 1),
                                                       wu_ap(e_, c + 1))
                                        elif c == 5:
                                            load_chunk(("wg", e_, tg, 6),
                                                       wg_ap(e_, 6))
                                            load_chunk(("wu", e_, tg, 6),
                                                       wu_ap(e_, 6))
                                            load_chunk(("wd", e_, tg, 0, 0),
                                                       wd_ap(e_, 0, 0))
                                            load_chunk(("wd", e_, tg, 0, 1),
                                                       wd_ap(e_, 0, 1))
                                        elif c == 6:
                                            load_chunk(("wg", e_, tg, 7),
                                                       wg_ap(e_, 7))
                                            load_chunk(("wu", e_, tg, 7),
                                                       wu_ap(e_, 7))
                                        elif c == 7:
                                            load_chunk(("wd", e_, tg, 0, 2),
                                                       wd_ap(e_, 0, 2))
                                            load_chunk(("wd", e_, tg, 0, 3),
                                                       wd_ap(e_, 0, 3))
                                    sg_, gdg = slot_ld[("wg", e_, tg, c)]
                                    su_, gdu = slot_ld[("wu", e_, tg, c)]
                                    fo = (f % 4) * P
                                    pb_ = f % 2
                                    wts = []
                                    if f == 0:
                                        wts += [("v", xet_done),
                                                ("v", prev_evac_all)]
                                    if f % 4 == 0:
                                        wts.append((f"gd{sg_}", gdg))
                                    if silu_hist[pb_]:
                                        wts.append(("s", silu_hist[pb_]))
                                    gl = 0
                                    for k in range(KT):
                                        gl = pr.op(
                                            "tensor", "matmul",
                                            (gp[:, pb_, 0:CAP],
                                             warena[:, sg_ * 8 + k, fo:fo + P],
                                             xet[:, k, :]),
                                            dict(start=(k == 0),
                                                 stop=(k == KT - 1)),
                                            waits=(wts if k == 0 else []),
                                            inc="t")
                                    wtu = []
                                    if f == 0:
                                        wtu.append(("v", prev_evac_all))
                                    if f % 4 == 0:
                                        wtu.append((f"gd{su_}", gdu))
                                    if h_hist[pb_]:
                                        wtu.append(("v", h_hist[pb_]))
                                    ul = 0
                                    for k in range(KT):
                                        ul = pr.op(
                                            "tensor", "matmul",
                                            (up[:, pb_, 0:CAP],
                                             warena[:, su_ * 8 + k, fo:fo + P],
                                             xet[:, k, :]),
                                            dict(start=(k == 0),
                                                 stop=(k == KT - 1)),
                                            waits=(wtu if k == 0 else []),
                                            inc="t")
                                    sb = f % 2
                                    ws = [("t", gl)]
                                    if prev_h_v:
                                        ws.append(("v", prev_h_v))
                                    prev_silu_s = pr.op(
                                        "scalar", "activation",
                                        (sg[:, sb, :], gp[:, pb_, 0:CAP],
                                         AF.Silu),
                                        waits=ws, inc="s")
                                    silu_hist[pb_] = prev_silu_s
                                    prev_h_v = pr.op(
                                        "vector", "tensor_tensor",
                                        (ht[:, f, :], sg[:, sb, :],
                                         up[:, pb_, 0:CAP], ALU.mult),
                                        waits=[("s", prev_silu_s), ("t", ul)],
                                        inc="v")
                                    h_hist[pb_] = prev_h_v
                                    if f % 4 == 3:
                                        cdone = f // 4
                                        for kind in ("wg", "wu"):
                                            sl_, _ = slot_ld.pop(
                                                (kind, e_, tg, cdone))
                                            slot_busy_t[sl_] = C["t"]
                            return C["v"]

                        def emit_hd_scatter(e_, h_all_v, ps_done,
                                            guard_v=0):
                            nonlocal prev_hds_rd
                            eb2 = e_ % 2
                            psx = psb[eb2]
                            with ExitStack() as hes:
                                hdp = hes.enter_context(
                                    nc.psum_tensor(f"hp_{tag}{tg}{e_}",
                                                   [P, NST, 512], F32))
                                spp = hes.enter_context(
                                    nc.psum_tensor(f"sp_{tag}{tg}{e_}",
                                                   [P, 2, 512], F32))
                                after_chunk = {
                                    (0, 0): [("wd", e_, tg, 1, 0)],
                                    (0, 1): [("wd", e_, tg, 1, 1)],
                                    (0, 2): [("wd", e_, tg, 1, 2)],
                                    (0, 3): [("wd", e_, tg, 1, 3)],
                                    (1, 0): ([("wg", e_ + 1, tg, 0)]
                                             if e_ + 1 < ne else []),
                                    (1, 1): ([("wu", e_ + 1, tg, 0)]
                                             if e_ + 1 < ne else []),
                                }
                                prev_sct = [0, 0]
                                hds_ev = 0
                                pending_sc = []
                                for dc in range(2):
                                    evac_t = {}
                                    for c in range(4):
                                        sd_, gdd = slot_ld[("wd", e_, tg, dc, c)]
                                        if dc == 1 and c == 1 and pending_sc:
                                            for fn in pending_sc:
                                                fn()
                                            pending_sc.clear()
                                        for st in range(NST):
                                            w = []
                                            if st == 0:
                                                w.append((f"gd{sd_}", gdd))
                                            if c == 0:
                                                w.append(("v", h_all_v))
                                                if dc == 0 and st == 0:
                                                    w.append(("v", guard_v))
                                                if dc == 1:
                                                    w.append(("v", hds_ev))
                                            hl = 0
                                            for fi in range(KT):
                                                hl = pr.op(
                                                    "tensor", "matmul",
                                                    (hdp[:, st, :],
                                                     ht[:, c * 8 + fi,
                                                        st * P:(st + 1) * P],
                                                     warena[:, sd_ * 8 + fi, :]),
                                                    dict(start=(c == 0 and
                                                                fi == 0),
                                                         stop=(c == 3 and
                                                               fi == KT - 1)),
                                                    waits=(w if fi == 0 else []),
                                                    inc="t")
                                            if c == 3:
                                                evac_t[st] = hl
                                        sl_, _ = slot_ld.pop(("wd", e_, tg,
                                                              dc, c))
                                        slot_busy_t[sl_] = C["t"]
                                        for key in after_chunk.get((dc, c), []):
                                            if key[0] == "wd":
                                                load_chunk(key,
                                                           wd_ap(key[1], key[3],
                                                                 key[4]))
                                            elif key[0] == "wg":
                                                load_chunk(key, wg_ap(key[1], 0))
                                            else:
                                                load_chunk(key, wu_ap(key[1], 0))
                                    w = [("t", evac_t[NST - 1])]
                                    if prev_hds_rd:
                                        w.append(("t", prev_hds_rd))
                                    hds_ev = pr.op(
                                        "vector", "tensor_copy",
                                        (hds[:, :, :], hdp[:, :, :]),
                                        waits=w, inc="v")
                                    def emit_scatter(dc=dc, hds_ev=hds_ev,
                                                     hdslo=0):
                                        nonlocal prev_hds_rd
                                        for tt in range(8):
                                            t_ = tg * 8 + tt
                                            sb2 = tt % 2
                                            sl = 0
                                            for st in range(NST):
                                                w = []
                                                if st == 0:
                                                    w = [("v", hds_ev),
                                                         ("v", ps_done),
                                                         ("v", prev_sct[sb2])]
                                                sl = pr.op(
                                                    "tensor", "matmul",
                                                    (spp[:, sb2, :],
                                                     psx[:, tt * NST + st,
                                                         hdslo:hdslo + P]
                                                     if False else
                                                     psx[:, tt * NST + st, :],
                                                     hds[:, hdslo + st, :]),
                                                    dict(start=(st == 0),
                                                         stop=(st == NST - 1)),
                                                    waits=w, inc="t")
                                            prev_sct[sb2] = pr.op(
                                                "vector",
                                                "scalar_tensor_tensor",
                                                (xs[:, t_,
                                                    dc * 512:(dc + 1) * 512],
                                                 spp[:, sb2, :],
                                                 cwb[:, t_, e_:e_ + 1],
                                                 xs[:, t_,
                                                    dc * 512:(dc + 1) * 512],
                                                 ALU.mult, ALU.add),
                                                waits=[("t", sl)], inc="v")
                                        prev_hds_rd = C["t"]
                                    if dc == 0:
                                        pending_sc.append(emit_scatter)
                                    else:
                                        emit_scatter()
                            ps_rd_t[eb2] = C["t"]
                            return max(prev_sct)

                        pg0 = emit_build_dve(0)
                        bstate[0] = (pg0, emit_ps_transposes(0, pg0, 0))
                        for e_ in range(ne):
                            pg_done, ps_done = bstate.pop(e_)
                            xet_done = emit_gather(e_, pg_done, ps_done)
                            if e_ + 1 < ne:
                                npg = emit_build_dve(e_ + 1)
                            h_all_v = emit_gu(e_, xet_done, prev_evac_all)
                            guard = 0
                            if e_ + 1 < ne:
                                nps = emit_ps_transposes(e_ + 1, npg, h_all_v)
                                bstate[e_ + 1] = (npg, nps)
                                guard = nps
                            prev_evac_all = emit_hd_scatter(e_, h_all_v,
                                                            ps_done, guard)


        def ln2_phase():
            frame_barrier()
            x_ready_v = C["v"]
            with ExitStack() as fes:
                yst = fes.enter_context(nc.sbuf_tensor("yst_l2", [P, 2, D], F32))
                ybf = fes.enter_context(nc.sbuf_tensor("ybf_l2", [P, 2, D], BF16))
                prev_dma = [0, 0]
                stats_v = {}
                ab = {}
                pssq = 0
                for t in range(NT):
                    ab[t] = ln_stats_a(t, yst[:, 0, :], x_ready_v, pssq)
                    pssq = ab[t][1]
                for t in range(NT):
                    stats_v[t] = ln_stats_b(t, *ab[t])
                for t in range(NT):
                    buf = t % 2
                    v8, _ = ln_norm(t, stats_v[t], yst[:, buf, :],
                                    ybf[:, buf, :], None,
                                    extra_v8=[(f"ldc{buf}", prev_dma[buf])])
                    prev_dma[buf] = pr.op(
                        "sync", "dma_start",
                        kwargs=dict(out=cc_in3[:, t, :], in_=ybf[:, buf, :]),
                        waits=[("v", v8)], inc=f"ldc{buf}", incby=16)

        def attention():
            ln2_phase()
            pr.op("gpsimd", "collective_compute",
                  ("AllGather", ALU.bypass, [[0, 1], [2, 3], [4, 5], [6, 7]]),
                  dict(ins=[cc_in.ap().opt()], outs=[cc_out.ap().opt()]),
                  waits=[("ldc0", C["ldc0"]), ("ldc1", C["ldc1"])], inc="cc")

            frame_barrier()
            with ExitStack() as aes:
                y2f = aes.enter_context(nc.sbuf_tensor("y2f", [P, NGT, D], BF16))
                y2hhT = [aes.enter_context(
                    nc.sbuf_tensor(f"y2hhT{i}", [P, 4096], BF16))
                    for i in range(2)]
                yhK = [aes.enter_context(nc.sbuf_tensor(f"yhK{i}", [P, 4096],
                                                        BF16)) for i in range(2)]
                yhQ = [aes.enter_context(nc.sbuf_tensor(f"yhQ{i}", [P, 4096],
                                                        BF16)) for i in range(2)]
                va = [aes.enter_context(nc.sbuf_tensor(f"vA{i}", [P, NGT, 66],
                                                       BF16)) for i in range(2)]
                ysq = aes.enter_context(nc.sbuf_tensor("ysq", [P, 1024], BF16))
                masks_s = aes.enter_context(
                    nc.sbuf_tensor("masks_s", [P, 8, 512], BF16))
                pT = [aes.enter_context(nc.sbuf_tensor(f"pT{i}", [P, 2, 512],
                                                       BF16)) for i in range(3)]
                otsb = [aes.enter_context(nc.sbuf_tensor(f"ots{i}", [P, 512],
                                                         F32)) for i in range(2)]
                rcps = aes.enter_context(nc.sbuf_tensor("rcps", [P, 4], F32))

                pr.op("sync", "dma_start",
                      kwargs=dict(out=masks_s[:, :, :], in_=masks_h[:, :, :]),
                      inc="ldm", incby=16)
                for gb in range(8):
                    start = (gb % 2) * 2048 + (gb // 2) * 512
                    src = cc_out.ap()[start:start + 512, :].rearrange(
                        "(t p) d -> p t d", p=P)
                    pr.op("sync", "dma_start",
                          kwargs=dict(out=y2f[:, gb * 4:(gb + 1) * 4, :],
                                      in_=src),
                          waits=[("cc", 1)], inc="ldy", incby=16)
                y2_ld = C["ldy"]

                pair_rd_t = [0, 0]
                yh_rd_t = [0, 0]
                va_rd_t = [0, 0]
                ot_evac_v = 0
                tr2_rd_v = 0
                ysq_rd_t = 0
                bq_done_prev = 0
                pending = []
                s_bar = 0
                v_bar = 0
                NLOOK = 2
                NSB = 3

                for hp2 in range(8):
                    pb = hp2 % 2
                    with ExitStack() as tfr:
                        trps = [tfr.enter_context(
                            nc.psum_tensor(f"atrp{hp2}{i}", [P, 4, P], BF16))
                            for i in range(2)]
                        tr_busy = [0, 0]
                        cps = []
                        for g4 in range(8):
                            bank = g4 % 2
                            tc_ = 0
                            for j in range(4):
                                gt = g4 * 4 + j
                                w = [("v", v_idbf), ("ldy", y2_ld)]
                                if j == 0:
                                    w.append(("v", tr_busy[bank]))
                                if gt == 0:
                                    w += [("t", pair_rd_t[pb]),
                                          ("s", s_bar), ("v", v_bar)]
                                tc_ = pr.op("tensor", "transpose",
                                            (trps[bank][:, j, :],
                                             y2f[:, gt, hp2 * P:(hp2 + 1) * P],
                                             idbf[:, :]),
                                            waits=w, inc="t")
                            vc = pr.op("vector", "tensor_copy",
                                       (y2hhT[pb][:, g4 * 512:(g4 + 1) * 512],
                                        trps[bank][:, :, :]),
                                       waits=[("t", tc_)], inc="v")
                            tr_busy[bank] = vc
                            cps.append(vc)
                        pair_v = max(cps)

                    with ExitStack() as pes:
                        spps = [pes.enter_context(
                            nc.psum_tensor(f"aspp{hp2}{i}", [P, 2, 512], F32))
                            for i in range(NSB)]
                        otps = pes.enter_context(
                            nc.psum_tensor(f"aotp{hp2}", [P, 512], F32))
                        trp2 = pes.enter_context(
                            nc.psum_tensor(f"atr2{hp2}", [P, P], F32))
                        sp_cons = [("s", 0)] * NSB

                        for hi in range(2):
                            h = hp2 * 2 + hi
                            yb = h % 2
                            wprev = [("t", yh_rd_t[yb])]
                            vK = pr.op("vector", "tensor_copy",
                                       (yhK[yb][0:HD, :],
                                        y2hhT[pb][hi * HD:(hi + 1) * HD, :]),
                                       waits=[("v", pair_v)] + wprev, inc="v")
                            vK2 = pr.op("vector", "memset",
                                        (yhK[yb][HD:HD + 1, :], -1.0),
                                        waits=wprev, inc="v")
                            vQ = pr.op("vector", "tensor_copy",
                                       (yhQ[yb][0:HD, :],
                                        y2hhT[pb][hi * HD:(hi + 1) * HD, :]),
                                       waits=[("v", pair_v)] + wprev, inc="v")
                            bq_chunk = []
                            bq_last = [bq_done_prev, bq_done_prev]
                            spq_used = 0
                            for hq in range(4):
                                sqs = pr.op("scalar", "activation",
                                            (ysq[:, :],
                                             y2hhT[pb][:, hq * 1024:
                                                       (hq + 1) * 1024],
                                             AF.Square),
                                            waits=[("v", pair_v),
                                                   ("t", ysq_rd_t)], inc="s")
                                for ch in range(2):
                                    gci = hq * 2 + ch
                                    par = gci % 2
                                    q0 = hq * 1024 + ch * 512
                                    tgt = (otps[0:1, :] if par == 0
                                           else spps[0][0:1, 0, :])
                                    w = [("s", sqs), ("v", v_onec),
                                         ("s", bq_last[par])]
                                    if par == 0:
                                        w.append(("v", ot_evac_v))
                                    else:
                                        w.append(sp_cons[0])
                                    bt = pr.op(
                                        "tensor", "matmul",
                                        (tgt,
                                         onecol[hi * HD:(hi + 1) * HD, :],
                                         ysq[hi * HD:(hi + 1) * HD,
                                             ch * 512:(ch + 1) * 512]),
                                        dict(start=True, stop=True),
                                        waits=w, inc="t")
                                    bq_last[par] = pr.op(
                                        "scalar", "activation",
                                        (yhQ[yb][HD:HD + 1, q0:q0 + 512],
                                         tgt, AF.Sqrt),
                                        dict(scale=1024.0),
                                        waits=[("t", bt), ("v", vQ)], inc="s")
                                    bq_chunk.append(bq_last[par])
                                    if par == 1:
                                        spq_used = bq_last[par]
                                ysq_rd_t = C["t"]
                            bq_done_prev = max(bq_last)
                            bq_sp_guard = spq_used
                            vb_ = h % 2
                            wva = [("ldy", y2_ld), ("t", va_rd_t[vb_])]
                            va1 = pr.op("vector", "tensor_copy",
                                        (va[vb_][:, :, 0:HD],
                                         y2f[:, :, h * HD:(h + 1) * HD]),
                                        waits=wva, inc="v")
                            va2 = pr.op("vector", "memset",
                                        (va[vb_][:, :, HD:HD + 1], 1.0),
                                        waits=wva, inc="v")
                            va_done = max(va1, va2)

                            for s_ in range(SLOTS):
                                nkt = 8 * s_ + 8
                                G = nkt // 2
                                bq_gate = bq_chunk[min(2 * s_ + 1, 7)]

                                def emit_qk(g, s_=s_, yb=yb, bq_gate=bq_gate,
                                            vK=vK, vK2=vK2):
                                    sb = g % NSB
                                    qk = 0
                                    for j in range(2):
                                        kt = 2 * g + j
                                        wqk = [("v", vK), ("v", vK2),
                                               ("s", bq_gate)]
                                        if g == 0 and s_ == 0 and j == 0:
                                            wqk.append(("s", bq_sp_guard))
                                        if j == 0:
                                            wqk.append(sp_cons[sb])
                                        qk = pr.op(
                                            "tensor", "matmul",
                                            (spps[sb][:, j, :],
                                             yhK[yb][0:HD + 1,
                                                     kt * P:(kt + 1) * P],
                                             Dyn(lambda e, yb=yb, s_=s_:
                                                 yhQ[yb][0:HD + 1, bass.ds(
                                                     qoff_expr(e, s_), 512)])),
                                            dict(start=True, stop=True),
                                            waits=wqk, inc="t")
                                    return qk

                                qk_cnt = {}
                                for g0 in range(min(NLOOK, G)):
                                    qk_cnt[g0] = emit_qk(g0)
                                    if g0 == 0:
                                        for fn in pending:
                                            fn()
                                        pending.clear()
                                pv_last = 0
                                for g in range(G):
                                    sb = g % NSB
                                    ex = pr.op("scalar", "activation",
                                               (pT[sb][:, :, :],
                                                spps[sb][:, :, :],
                                                AF.Exp), dict(scale=0.125),
                                               waits=[("t", qk_cnt[g])],
                                               inc="s")
                                    sp_cons[sb] = ("s", ex)
                                    pt_ready = [("s", ex), ("s", ex)]
                                    for j in range(2):
                                        kt = 2 * g + j
                                        if kt >= nkt - 8:
                                            jj = kt - (nkt - 8)
                                            mv = pr.op(
                                                "vector", "tensor_tensor",
                                                (pT[sb][:, j, :],
                                                 pT[sb][:, j, :],
                                                 masks_s[:, jj, :], ALU.mult),
                                                waits=[("s", ex), ("ldm", 16)],
                                                inc="v")
                                            sp_cons[sb] = ("v", mv)
                                            pt_ready[j] = ("v", mv)
                                    if g + NLOOK < G:
                                        qk_cnt[g + NLOOK] = emit_qk(g + NLOOK)
                                    for j in range(2):
                                        kt = 2 * g + j
                                        wpv = [pt_ready[j], ("v", va_done)]
                                        if kt == 0:
                                            wpv.append(("v", ot_evac_v))
                                        pv_last = pr.op(
                                            "tensor", "matmul",
                                            (otps[0:HD + 1, :],
                                             va[vb_][:, kt, 0:HD + 1],
                                             pT[sb][:, j, :]),
                                            dict(start=(kt == 0),
                                                 stop=(kt == nkt - 1)),
                                            waits=wpv, inc="t")
                                ob = s_ % 2
                                ev = pr.op("vector", "tensor_copy",
                                           (otsb[ob][0:HD + 1, :],
                                            otps[0:HD + 1, :]),
                                           waits=[("t", pv_last)], inc="v")
                                ot_evac_v = ev

                                def deferred(ev=ev, ob=ob, s_=s_, h=h,
                                             trp2=trp2):
                                    nonlocal tr2_rd_v
                                    for q4 in range(4):
                                        t2 = pr.op(
                                            "tensor", "transpose",
                                            (trp2[0:P, 0:HD + 1],
                                             otsb[ob][0:HD + 1,
                                                      q4 * P:(q4 + 1) * P],
                                             id32[0:HD + 1, 0:HD + 1]),
                                            waits=[("v", ev), ("v", tr2_rd_v)],
                                            inc="t")
                                        rc = pr.op("vector", "reciprocal",
                                                   (rcps[:, q4:q4 + 1],
                                                    trp2[:, HD:HD + 1]),
                                                   waits=[("t", t2)], inc="v")
                                        tr2_rd_v = pr.op(
                                            "vector", "scalar_tensor_tensor",
                                            (xs[:, s_ * 4 + q4,
                                                h * HD:(h + 1) * HD],
                                             trp2[:, 0:HD], rcps[:, q4:q4 + 1],
                                             xs[:, s_ * 4 + q4,
                                                h * HD:(h + 1) * HD],
                                             ALU.mult, ALU.add),
                                            waits=[("v", rc)], inc="v")
                                pending.append(deferred)
                            yh_rd_t[yb] = C["t"]
                            va_rd_t[vb_] = C["t"]
                        for fn in pending:
                            fn()
                        pending.clear()
                        pair_rd_t[pb] = C["t"]
                        s_bar = C["s"]
                        v_bar = C["v"]

        if CFG["DO_MOE1"]:
            moe_layer("m1")
        if CFG["DO_ATT"]:
            attention()
        if CFG["DO_MOE3"]:
            moe_layer("m3")

        pr.op("sync", "dma_start", kwargs=dict(out=x_out3, in_=xs[:, :, :]),
              waits=[("v", C["v"]), ("s", C["s"]), ("t", C["t"])],
              inc="st", incby=16)
        stf = C["st"]

        block = es.enter_context(nc.Block())

        @block.sync
        def _(sync):
            pr.emit(sync, "sync", sems)
            sync.wait_ge(sems["st"], stf)

        @block.gpsimd
        def _(gpsimd):
            pr.emit(gpsimd, "gpsimd", sems)

        @block.vector
        def _(vector):
            pr.emit(vector, "vector", sems)

        @block.scalar
        def _(scalar):
            pr.emit(scalar, "scalar", sems)

        @block.tensor
        def _(tensor):
            pr.emit(tensor, "tensor", sems)

    return nc


def make_masks(parity):
    k = np.arange(P)[:, None]
    q = np.arange(512)[None, :]
    m = np.zeros((P, 8, 512), dtype=ml_dtypes.bfloat16)
    for j in range(8):
        if parity == 0:
            if j < 4:
                m[:, j, :] = (q >= 128 * j + k).astype(ml_dtypes.bfloat16)
        else:
            if j < 4:
                m[:, j, :] = 1
            else:
                m[:, j, :] = (q >= 128 * (j - 4) + k).astype(ml_dtypes.bfloat16)
    return m


def kernel(L0_attn_x, L0_ffn_router_w, L0_ffn_w_gate, L0_ffn_w_up, L0_ffn_w_down):
    x = np.ascontiguousarray(np.asarray(L0_attn_x, dtype=np.float32))
    rw = np.ascontiguousarray(np.asarray(L0_ffn_router_w, dtype=np.float32))
    wg = np.ascontiguousarray(np.asarray(L0_ffn_w_gate, dtype=np.float32))
    wu = np.ascontiguousarray(np.asarray(L0_ffn_w_up, dtype=np.float32))
    wd = np.ascontiguousarray(np.asarray(L0_ffn_w_down, dtype=np.float32))
    ident = np.eye(P, dtype=np.float32)
    tril = (np.arange(P)[:, None] < np.arange(P)[None, :]).astype(
        ml_dtypes.bfloat16)
    iota = np.broadcast_to(np.arange(384, dtype=np.float16), (P, 384)).copy()

    nc = build_nc()
    in_maps = []
    for c in range(8):
        b, p = c // 2, c % 2
        shard = np.concatenate(
            [x[b, (2 * s + p) * 512:(2 * s + p + 1) * 512, :] for s in range(4)],
            axis=0)
        in_maps.append({
            "x_in": np.ascontiguousarray(shard),
            "rw": rw, "wg": wg, "wu": wu, "wd": wd,
            "masks": make_masks(p), "ident32": ident,
            "trilput": tril, "iota384": iota,
        })
    res = run_bass_kernel_spmd(nc, in_maps, core_ids=list(range(8)))
    out = np.empty((4, 4096, D), dtype=np.float32)
    for c in range(8):
        b, p = c // 2, c % 2
        shard = np.asarray(res.results[c]["x_outp"])
        for s in range(4):
            out[b, (2 * s + p) * 512:(2 * s + p + 1) * 512, :] = \
                shard[s * 512:(s + 1) * 512, :]
    return out


# version 09 (final)
# speedup vs baseline: 1.0757x; 1.0757x over previous
import numpy as np
import ml_dtypes
import concourse.bass as bass
import concourse.mybir as mybir
from concourse.bass_utils import run_bass_kernel_spmd
from contextlib import ExitStack

F32 = mybir.dt.float32
BF16 = mybir.dt.bfloat16
AF = mybir.ActivationFunctionType
ALU = mybir.AluOpType
AX = mybir.AxisListType

P = 128
D = 1024
KT = 8
NT = 16
NTOK = 2048
NE = 8
DFF = 4096
NF = 32
H = 16
HD = 64
NGT = 32
SLOTS = 4
EPS = 1e-5

CFG = {"NE": 8, "DO_MOE1": True, "DO_ATT": True, "DO_MOE3": True}

ENGS = ("sync", "gpsimd", "tensor", "vector", "scalar")


class Dyn:

    def __init__(self, fn):
        self.fn = fn


_QOFF_CACHE = {}


def qoff_expr(e, s_):
    key = (id(e), s_)
    if key not in _QOFF_CACHE:
        pk = (id(e), "pid")
        if pk not in _QOFF_CACHE:
            _QOFF_CACHE[pk] = e.partition_id() % 2
        _QOFF_CACHE[key] = _QOFF_CACHE[pk] * 512 + s_ * 1024
    return _QOFF_CACHE[key]


class Prog:
    def __init__(self):
        self.q = {e: [] for e in ENGS}
        self.cnt = {}

    def sem(self, name):
        self.cnt[name] = 0

    def op(self, eng, method, args=(), kwargs=None, waits=(), inc=None, incby=1):
        waits = tuple((s, v) for (s, v) in waits if v)
        self.q[eng].append((method, tuple(args), dict(kwargs or {}), waits, inc,
                            incby))
        if inc is not None:
            self.cnt[inc] += incby
            return self.cnt[inc]
        return None

    def emit(self, handle, eng, sems):
        for method, args, kwargs, waits, inc, incby in self.q[eng]:
            for s, v in waits:
                handle.wait_ge(sems[s], v)
            a = [x.fn(handle) if isinstance(x, Dyn) else x for x in args]
            kw = {k: (x.fn(handle) if isinstance(x, Dyn) else x)
                  for k, x in kwargs.items()}
            inst = getattr(handle, method)(*a, **kw)
            if inc is not None:
                inst.then_inc(sems[inc], incby)


def build_nc():
    nc = bass.Bass("TRN2", target_bir_lowering=False)
    ne = CFG["NE"]

    x_in = nc.dram_tensor("x_in", [NTOK, D], F32, kind="ExternalInput")
    rw_h = nc.dram_tensor("rw", [NE, D], F32, kind="ExternalInput")
    wg_h = nc.dram_tensor("wg", [NE, D, DFF], F32, kind="ExternalInput")
    wu_h = nc.dram_tensor("wu", [NE, D, DFF], F32, kind="ExternalInput")
    wd_h = nc.dram_tensor("wd", [NE, DFF, D], F32, kind="ExternalInput")
    masks_h = nc.dram_tensor("masks", [P, 8, 512], BF16, kind="ExternalInput")
    id_h = nc.dram_tensor("ident32", [P, P], F32, kind="ExternalInput")
    tril_h = nc.dram_tensor("trilput", [P, P], BF16, kind="ExternalInput")
    iota_h = nc.dram_tensor("iota384", [P, 384], mybir.dt.float16, kind="ExternalInput")
    x_out = nc.dram_tensor("x_outp", [NTOK, D], F32, kind="ExternalOutput")
    cc_in = nc.dram_tensor("cc_in", [NTOK, D], BF16, kind="Internal")
    cc_out = nc.dram_tensor("cc_out", [2 * NTOK, D], BF16, kind="Internal")

    x_in3 = x_in.ap().rearrange("(t p) d -> p t d", p=P)
    x_out3 = x_out.ap().rearrange("(t p) d -> p t d", p=P)
    cc_in3 = cc_in.ap().rearrange("(t p) d -> p t d", p=P)

    pr = Prog()
    SEMS = ("ldx", "ldi", "ldr", "ldm", "ldy", "ldc0", "ldc1", "ldt", "ldo",
            "t", "v", "s", "cc", "st") + tuple(f"gd{i}" for i in range(6))
    for s in SEMS:
        pr.sem(s)
    C = pr.cnt

    es = ExitStack()
    with es:
        sems = {s: es.enter_context(nc.semaphore("sem_" + s)) for s in SEMS}

        xs = es.enter_context(nc.sbuf_tensor("xs", [P, NT, D], F32))
        id32 = es.enter_context(nc.sbuf_tensor("id32", [P, P], F32))
        idbf = es.enter_context(nc.sbuf_tensor("idbf", [P, P], BF16))
        rwTh = es.enter_context(nc.sbuf_tensor("rwTh", [P, KT, NE], BF16))
        rwTr = es.enter_context(nc.sbuf_tensor("rwTr", [P, KT, NE], BF16))
        cwb = es.enter_context(nc.sbuf_tensor("cwb", [P, NT, NE], F32))
        sc = es.enter_context(nc.sbuf_tensor("scs", [P, NT, 12], F32))
        onecol = es.enter_context(nc.sbuf_tensor("onecol", [P, 1], BF16))
        epscol = es.enter_context(nc.sbuf_tensor("epscol", [P, 1], F32))
        estmp = es.enter_context(nc.sbuf_tensor("estmp", [P, 2, NE], F32))
        estmp2 = es.enter_context(nc.sbuf_tensor("estmp2", [P, 2, NE], F32))
        trilput = es.enter_context(nc.sbuf_tensor("trilput_s", [P, P], BF16))
        iota384 = es.enter_context(nc.sbuf_tensor("iota384_s", [P, 384], mybir.dt.float16))

        pr.op("sync", "dma_start", kwargs=dict(out=xs[:, :, :], in_=x_in3),
              inc="ldx", incby=16)
        pr.op("sync", "dma_start", kwargs=dict(out=id32[:, :], in_=id_h[:, :]),
              inc="ldi", incby=16)
        v_idbf = pr.op("vector", "tensor_copy", (idbf[:, :], id32[:, :]),
                       waits=[("ldi", 16)], inc="v")
        v_onec = pr.op("vector", "memset", (onecol[:, :], 1.0), inc="v")
        v_eps = pr.op("vector", "memset", (epscol[:, :], EPS), inc="v")
        pr.op("sync", "dma_start", kwargs=dict(out=trilput[:, :],
                                               in_=tril_h[:, :]),
              inc="ldt", incby=16)
        pr.op("sync", "dma_start", kwargs=dict(out=iota384[:, :],
                                               in_=iota_h[:, :]),
              inc="ldo", incby=16)

        with ExitStack() as ies:
            rws = ies.enter_context(nc.sbuf_tensor("rws", [NE, D], F32))
            rwT32 = ies.enter_context(nc.sbuf_tensor("rwT32", [P, KT, NE], F32))
            trp32 = ies.enter_context(nc.psum_tensor("trp32", [P, NE], F32))
            pr.op("sync", "dma_start", kwargs=dict(out=rws[:, :], in_=rw_h[:, :]),
                  inc="ldr", incby=16)
            prev_cp = 0
            for k in range(KT):
                tc_ = pr.op("tensor", "transpose",
                            (trp32[:, :], rws[0:NE, k * P:(k + 1) * P],
                             id32[0:NE, 0:NE]),
                            waits=[("ldr", 16), ("ldi", 16), ("v", prev_cp)],
                            inc="t")
                prev_cp = pr.op("vector", "tensor_copy",
                                (rwT32[:, k, :], trp32[:, :]),
                                waits=[("t", tc_)], inc="v")
            vh = pr.op("vector", "tensor_copy", (rwTh[:, :, :], rwT32[:, :, :]),
                       waits=[("v", prev_cp)], inc="v")
            v_rw_done = pr.op("vector", "tensor_tensor",
                              (rwTr[:, :, :], rwT32[:, :, :], rwTh[:, :, :],
                               ALU.subtract),
                              waits=[("v", vh)], inc="v")

        def frame_barrier():
            snap = [(s, C[s]) for s in SEMS if C[s] > 0]
            for eng in ENGS:
                pr.op(eng, "nop", waits=snap)

        def ln_stats_a(t, dump, x_ready_v, prev_ssq=0):
            s_ = sc[:, t, :]
            vsum = pr.op("vector", "reduce_sum", (s_[:, 0:1], xs[:, t, :]),
                         dict(axis=AX.X),
                         waits=[("ldx", 16), ("v", x_ready_v)], inc="v")
            ssq = pr.op("scalar", "activation", (dump[:, :], xs[:, t, :],
                                                 AF.Square),
                        dict(accum_out=s_[:, 1:2]),
                        waits=[("ldx", 16), ("v", x_ready_v),
                               ("s", prev_ssq)], inc="s")
            return vsum, ssq

        def ln_stats_b(t, vsum, ssq):
            s_ = sc[:, t, :]
            v1 = pr.op("vector", "tensor_scalar",
                       (s_[:, 2:3], s_[:, 0:1], -1.0 / D, None, ALU.mult),
                       waits=[("v", vsum)], inc="v")
            v2 = pr.op("vector", "tensor_tensor",
                       (s_[:, 3:4], s_[:, 2:3], s_[:, 2:3], ALU.mult),
                       waits=[("v", v1)], inc="v")
            v3 = pr.op("vector", "tensor_scalar",
                       (s_[:, 4:5], s_[:, 1:2], 1.0 / D, None, ALU.mult),
                       waits=[("s", ssq), ("v", v2)], inc="v")
            v4 = pr.op("vector", "tensor_tensor",
                       (s_[:, 5:6], s_[:, 4:5], s_[:, 3:4], ALU.subtract),
                       waits=[("v", v3)], inc="v")
            s1 = pr.op("scalar", "activation", (s_[:, 6:7], s_[:, 5:6], AF.Sqrt),
                       dict(bias=epscol[:, :]),
                       waits=[("v", v4), ("v", v_eps)], inc="s")
            v5 = pr.op("vector", "reciprocal", (s_[:, 7:8], s_[:, 6:7]),
                       waits=[("s", s1)], inc="v")
            v6 = pr.op("vector", "tensor_tensor",
                       (s_[:, 3:4], s_[:, 2:3], s_[:, 7:8], ALU.mult),
                       waits=[("v", v5)], inc="v")
            return v6

        def ln_norm(t, stats_v, y32, yb, yres, extra_v7=(), extra_v8=(),
                    extra_v9=()):
            s_ = sc[:, t, :]
            v7 = pr.op("vector", "tensor_scalar",
                       (y32[:, :], xs[:, t, :], s_[:, 7:8], s_[:, 3:4],
                        ALU.mult, ALU.add),
                       waits=[("v", stats_v)] + list(extra_v7), inc="v")
            v8 = pr.op("vector", "tensor_copy", (yb[:, :], y32[:, :]),
                       waits=[("v", v7)] + list(extra_v8), inc="v")
            v9 = v8
            if yres is not None:
                v9 = pr.op("vector", "tensor_tensor",
                           (yres[:, :], y32[:, :], yb[:, :], ALU.subtract),
                           waits=[("v", v8)] + list(extra_v9), inc="v")
            return v8, v9

        def moe_layer(tag):
            frame_barrier()
            x_ready_v = C["v"]
            CAP = 384
            NST = CAP // P
            fes = ExitStack()
            with fes:
                yst = fes.enter_context(nc.sbuf_tensor(f"yst_{tag}", [P, D], F32))
                yrs = fes.enter_context(nc.sbuf_tensor(f"yrs_{tag}", [P, D], BF16))
                ytokm = fes.enter_context(
                    nc.sbuf_tensor(f"ytokm_{tag}", [P, 8, D], BF16))
                ytg = fes.enter_context(
                    nc.sbuf_tensor(f"ytg_{tag}", [P, 2, KT, P], BF16))
                yrtg = fes.enter_context(
                    nc.sbuf_tensor(f"yrtg_{tag}", [P, KT, P], BF16))
                ht = fes.enter_context(
                    nc.sbuf_tensor(f"ht_{tag}", [P, NF, CAP], BF16))
                warena = fes.enter_context(
                    nc.sbuf_tensor(f"wa_{tag}", [P, 48, 512], BF16))
                sg = fes.enter_context(nc.sbuf_tensor(f"sg_{tag}", [P, 2, CAP],
                                                      BF16))
                xet = fes.enter_context(
                    nc.sbuf_tensor(f"xet_{tag}", [P, KT, CAP], BF16))
                hds = fes.enter_context(
                    nc.sbuf_tensor(f"hds_{tag}", [P, NST, 512], BF16))
                pgb = [fes.enter_context(
                    nc.sbuf_tensor(f"pg{i}_{tag}", [P, 8, CAP], BF16))
                    for i in range(2)]
                psb = [fes.enter_context(
                    nc.sbuf_tensor(f"ps{i}_{tag}", [P, 8 * NST, P], BF16))
                    for i in range(2)]
                scr = fes.enter_context(nc.sbuf_tensor(f"scr_{tag}", [P, 2, 48],
                                                       F32))
                rcbf = fes.enter_context(nc.sbuf_tensor(f"rcbf_{tag}", [P, 2, 1],
                                                        BF16))

                slot_busy_t = [0] * 6
                slot_ld = {}
                dma_i = [0]

                def load_chunk(key, src_ap):
                    if key in slot_ld:
                        return
                    slot = dma_i[0] % 6
                    dma_i[0] += 1
                    gdc = pr.op("gpsimd", "dma_start",
                                kwargs=dict(
                                    out=warena[:, slot * 8:(slot + 1) * 8, :],
                                    in_=src_ap),
                                waits=[("t", slot_busy_t[slot])],
                                inc=f"gd{slot}", incby=16)
                    slot_ld[key] = (slot, gdc)

                def wg_ap(e_, c):
                    return wg_h.ap()[e_, :, c * 512:(c + 1) * 512].rearrange(
                        "(k p) n -> p k n", p=P)

                def wu_ap(e_, c):
                    return wu_h.ap()[e_, :, c * 512:(c + 1) * 512].rearrange(
                        "(k p) n -> p k n", p=P)

                def wd_ap(e_, dc, c):
                    return wd_h.ap()[e_, c * 1024:(c + 1) * 1024,
                                     dc * 512:(dc + 1) * 512].rearrange(
                        "(k p) n -> p k n", p=P)

                ytokm_rd_t = 0
                yrs_rd_t = 0
                prev_evac_all = 0
                stats_v = {}
                ab = {}
                pssq = 0
                for t in range(NT):
                    ab[t] = ln_stats_a(t, yst, x_ready_v, pssq)
                    pssq = ab[t][1]
                for t in range(NT):
                    stats_v[t] = ln_stats_b(t, *ab[t])
                for tg in range(2):
                    tiles = list(range(tg * 8, tg * 8 + 8))
                    ytg_free_t = C["t"]
                    with ExitStack() as pes:
                        trps = [pes.enter_context(
                            nc.psum_tensor(f"trp_{tag}{tg}{i}", [P, 4, P],
                                           BF16)) for i in range(2)]
                        lps = pes.enter_context(
                            nc.psum_tensor(f"lp_{tag}{tg}", [P, NE], F32))
                        vbar = C["v"]
                        tr_busy = [vbar, vbar]
                        tri = [0]
                        prev_lps_rd = 0
                        yrtg_rd_t = 0
                        ytgb_rd_t = [ytg_free_t, ytg_free_t]
                        prev_es_rd = [0, 0]
                        prev_es2_rd = [0, 0]

                        def tr16(src, dst4s, vready, extra_copy_w):
                            last = 0
                            for g4 in range(2):
                                bank = tri[0] % 2
                                tri[0] += 1
                                tc_ = 0
                                for j in range(4):
                                    jj = g4 * 4 + j
                                    w = [("v", vready), ("v", v_idbf),
                                         ("t", ytg_free_t)]
                                    if j == 0:
                                        w.append(("v", tr_busy[bank]))
                                    tc_ = pr.op(
                                        "tensor", "transpose",
                                        (trps[bank][:, j, :],
                                         src[:, jj * P:(jj + 1) * P],
                                         idbf[:, :]),
                                        waits=w, inc="t")
                                vc = pr.op("vector", "tensor_copy",
                                           (dst4s[g4], trps[bank][:, :, :]),
                                           waits=[("t", tc_)] + extra_copy_w,
                                           inc="v")
                                tr_busy[bank] = vc
                                last = vc
                            return last

                        for t in tiles:
                            tt = t - tg * 8
                            tb = t % 2
                            v8, v9 = ln_norm(
                                t, stats_v[t], yst[:, :], ytokm[:, tt, :],
                                yrs[:, :],
                                extra_v8=[("t", ytokm_rd_t)],
                                extra_v9=[("t", yrs_rd_t)])
                            c1 = tr16(ytokm[:, tt, :],
                                      [ytg[:, tb, 0:4, :], ytg[:, tb, 4:8, :]],
                                      v8, [("t", ytgb_rd_t[tb])])
                            c2 = tr16(yrs[:, :],
                                      [yrtg[:, 0:4, :], yrtg[:, 4:8, :]],
                                      v9, [("t", yrtg_rd_t)])
                            yrs_rd_t = C["t"]
                            vready = max(c1, c2)
                            w0 = [("v", vready), ("v", v_rw_done),
                                  ("s", prev_lps_rd)]
                            last = 0
                            for ph in range(3):
                                rhs_t = rwTh if ph < 2 else rwTr
                                for k in range(KT):
                                    first = (ph == 0 and k == 0)
                                    stop = (ph == 2 and k == KT - 1)
                                    lhs = (yrtg[:, k, :] if ph == 1
                                           else ytg[:, tb, k, :])
                                    last = pr.op(
                                        "tensor", "matmul",
                                        (lps[:, :], lhs, rhs_t[:, k, :]),
                                        dict(start=first, stop=stop),
                                        waits=(w0 if first else []), inc="t")
                            yrtg_rd_t = last
                            ytgb_rd_t[tb] = last
                            s_ = sc[:, t, 8:12]
                            eb = t % 2
                            esl = estmp[:, eb, :]
                            es2 = estmp2[:, eb, :]
                            cv1 = pr.op("vector", "reduce_max",
                                        (s_[:, 0:1], lps[:, :]),
                                        dict(axis=AX.X, negate=True),
                                        waits=[("t", last)], inc="v")
                            ces = pr.op("scalar", "activation",
                                        (esl, lps[:, :], AF.Exp),
                                        dict(bias=s_[:, 0:1]),
                                        waits=[("v", cv1),
                                               ("v", prev_es_rd[eb])], inc="s")
                            prev_lps_rd = ces
                            cv2 = pr.op("vector", "tensor_scalar",
                                        (es2, esl, 1.0, None, ALU.is_lt),
                                        waits=[("s", ces),
                                               ("v", prev_es2_rd[eb])], inc="v")
                            cv3 = pr.op("vector", "tensor_tensor",
                                        (es2, esl, es2, ALU.mult),
                                        waits=[("v", cv2)], inc="v")
                            cv4 = pr.op("vector", "reduce_max",
                                        (s_[:, 1:2], es2), dict(axis=AX.X),
                                        waits=[("v", cv3)], inc="v")
                            cv5 = pr.op("vector", "tensor_scalar",
                                        (es2, esl, s_[:, 1:2], None, ALU.is_ge),
                                        waits=[("v", cv4)], inc="v")
                            cv6 = pr.op("vector", "tensor_tensor",
                                        (es2, esl, es2, ALU.mult),
                                        waits=[("v", cv5)], inc="v")
                            prev_es_rd[eb] = cv6
                            cv7 = pr.op("vector", "tensor_scalar",
                                        (s_[:, 2:3], s_[:, 1:2], 1.0, None,
                                         ALU.add),
                                        waits=[("v", cv6)], inc="v")
                            cv8 = pr.op("vector", "reciprocal",
                                        (s_[:, 3:4], s_[:, 2:3]),
                                        waits=[("v", cv7)], inc="v")
                            cv9 = pr.op("vector", "tensor_scalar",
                                        (cwb[:, t, :], es2, s_[:, 3:4], 0.5,
                                         ALU.mult, ALU.mult),
                                        waits=[("v", cv8)], inc="v")
                            prev_es2_rd[eb] = cv9
                    cw_done_v = C["v"]
                    ytokm_done_v = C["v"]

                    pg_rd_t = [0, 0]
                    ps_rd_t = [0, 0]
                    prev_hds_rd = 0
                    with ExitStack() as bfr:
                        bldcp = bfr.enter_context(
                            nc.psum_tensor(f"bcp_{tag}{tg}", [P, 1], F32))
                        ptr4 = [bfr.enter_context(
                            nc.psum_tensor(f"ptr_{tag}{tg}{i}", [P, 4, P],
                                           BF16)) for i in range(2)]
                        ptr_busy = [0, 0]
                        bld_prev_bv2 = [0]
                        bstate = {}

                        def emit_build_dve(e_):
                            eb2 = e_ % 2
                            pg = pgb[eb2]
                            s2 = scr[:, eb2, :]
                            bv0 = pr.op("vector", "tensor_scalar",
                                        (s2[:, 0:8],
                                         cwb[:, tg * 8:(tg + 1) * 8, e_],
                                         0.0, None, ALU.is_gt),
                                        waits=[("v", cw_done_v),
                                               ("t", pg_rd_t[eb2])], inc="v")
                            bv1 = pr.op("vector", "reduce_sum",
                                        (s2[:, 8:9], s2[:, 0:8]),
                                        dict(axis=AX.X),
                                        waits=[("v", bv0)], inc="v")
                            bv1b = pr.op("vector", "tensor_copy",
                                         (rcbf[:, eb2, :], s2[:, 8:9]),
                                         waits=[("v", bv1)], inc="v")
                            bt1 = pr.op("tensor", "matmul",
                                        (bldcp[:, :], trilput[:, :],
                                         rcbf[:, eb2, :]),
                                        dict(start=True, stop=True),
                                        waits=[("v", bv1b), ("ldt", 16),
                                               ("v", bld_prev_bv2[0])],
                                        inc="t")
                            bv2 = pr.op("vector", "tensor_copy",
                                        (s2[:, 9:10], bldcp[:, :]),
                                        waits=[("t", bt1)], inc="v")
                            bld_prev_bv2[0] = bv2
                            a_, b_ = 16, 24
                            pr.op("vector", "tensor_copy",
                                  (s2[:, a_:a_ + 8], s2[:, 0:8]),
                                  waits=[("v", bv0)], inc="v")
                            cur, oth = a_, b_
                            lastp = C["v"]
                            for kshift in (1, 2, 4):
                                p1 = pr.op("vector", "tensor_tensor",
                                           (s2[:, oth + kshift:oth + 8],
                                            s2[:, cur + kshift:cur + 8],
                                            s2[:, cur:cur + 8 - kshift],
                                            ALU.add),
                                           waits=[("v", lastp)], inc="v")
                                p2 = pr.op("vector", "tensor_copy",
                                           (s2[:, oth:oth + kshift],
                                            s2[:, cur:cur + kshift]),
                                           waits=[("v", lastp)], inc="v")
                                lastp = max(p1, p2)
                                cur, oth = oth, cur
                            bv3 = pr.op("vector", "tensor_tensor",
                                        (s2[:, 32:40], s2[:, cur:cur + 8],
                                         s2[:, 0:8], ALU.subtract),
                                        waits=[("v", lastp)], inc="v")
                            bv4 = pr.op("vector", "tensor_scalar",
                                        (s2[:, 32:40], s2[:, 32:40],
                                         s2[:, 9:10], None, ALU.add),
                                        waits=[("v", bv3), ("v", bv2)],
                                        inc="v")
                            pg_done = 0
                            for tt in range(8):
                                pg_done = pr.op(
                                    "vector", "tensor_scalar",
                                    (pg[:, tt, :], iota384[:, :],
                                     s2[:, 32 + tt:33 + tt],
                                     s2[:, tt:tt + 1],
                                     ALU.is_equal, ALU.mult),
                                    waits=[("v", bv4), ("ldo", 16)], inc="v")
                            return pg_done

                        def emit_ps_transposes(e_, pg_done):
                            eb2 = e_ % 2
                            pg = pgb[eb2]
                            psx = psb[eb2]
                            ps_done = 0
                            for g4 in range(6):
                                bank = g4 % 2
                                tpc = 0
                                for j in range(4):
                                    fl = g4 * 4 + j
                                    tt, st = fl // NST, fl % NST
                                    w = [("v", pg_done), ("v", v_idbf)]
                                    if j == 0:
                                        w.append(("v", ptr_busy[bank]))
                                    if fl == 0:
                                        w.append(("t", ps_rd_t[eb2]))
                                    tpc = pr.op(
                                        "tensor", "transpose",
                                        (ptr4[bank][:, j, :],
                                         pg[:, tt, st * P:(st + 1) * P],
                                         idbf[:, :]),
                                        waits=w, inc="t")
                                ps_done = pr.op(
                                    "vector", "tensor_copy",
                                    (psx[:, g4 * 4:(g4 + 1) * 4, :],
                                     ptr4[bank][:, :, :]),
                                    waits=[("t", tpc)], inc="v")
                                ptr_busy[bank] = ps_done
                            return ps_done

                        def emit_gather(e_, pg_done, ps_done):
                            eb2 = e_ % 2
                            pg = pgb[eb2]
                            with ExitStack() as ges:
                                gtp = [ges.enter_context(
                                    nc.psum_tensor(f"gt_{tag}{tg}{e_}{i}",
                                                   [P, CAP], F32))
                                    for i in range(2)]
                                gbusy = [0, 0]
                                xet_done = 0
                                for k8 in range(KT):
                                    bank = k8 % 2
                                    gl = 0
                                    for tt in range(8):
                                        w = []
                                        if tt == 0:
                                            w = [("v", pg_done),
                                                 ("v", ps_done),
                                                 ("v", ytokm_done_v),
                                                 ("v", gbusy[bank])]
                                        gl = pr.op(
                                            "tensor", "matmul",
                                            (gtp[bank][:, :],
                                             ytokm[:, tt, k8 * P:(k8 + 1) * P],
                                             pg[:, tt, :]),
                                            dict(start=(tt == 0),
                                                 stop=(tt == 7)),
                                            waits=w, inc="t")
                                    xet_done = pr.op(
                                        "vector", "tensor_copy",
                                        (xet[:, k8, :], gtp[bank][:, :]),
                                        waits=[("t", gl)], inc="v")
                                    gbusy[bank] = xet_done
                                pg_rd_t[eb2] = C["t"]
                            return xet_done

                        def emit_gu(e_, xet_done, prev_evac_all):
                            with ExitStack() as ges2:
                                gp = ges2.enter_context(
                                    nc.psum_tensor(f"gp_{tag}{tg}{e_}",
                                                   [P, 2, 512], F32))
                                up = ges2.enter_context(
                                    nc.psum_tensor(f"up_{tag}{tg}{e_}",
                                                   [P, 2, 512], F32))
                                silu_hist = [0, 0]
                                h_hist = [0, 0]
                                prev_silu_s = 0
                                prev_h_v = 0
                                for f in range(NF):
                                    c = f // 4
                                    if f % 4 == 0:
                                        if c == 0:
                                            for cc0 in (0, 1):
                                                load_chunk(("wg", e_, tg, cc0),
                                                           wg_ap(e_, cc0))
                                                load_chunk(("wu", e_, tg, cc0),
                                                           wu_ap(e_, cc0))
                                        elif c < 5:
                                            load_chunk(("wg", e_, tg, c + 1),
                                                       wg_ap(e_, c + 1))
                                            load_chunk(("wu", e_, tg, c + 1),
                                                       wu_ap(e_, c + 1))
                                        elif c == 5:
                                            load_chunk(("wg", e_, tg, 6),
                                                       wg_ap(e_, 6))
                                            load_chunk(("wu", e_, tg, 6),
                                                       wu_ap(e_, 6))
                                            load_chunk(("wd", e_, tg, 0, 0),
                                                       wd_ap(e_, 0, 0))
                                            load_chunk(("wd", e_, tg, 0, 1),
                                                       wd_ap(e_, 0, 1))
                                        elif c == 6:
                                            load_chunk(("wg", e_, tg, 7),
                                                       wg_ap(e_, 7))
                                            load_chunk(("wu", e_, tg, 7),
                                                       wu_ap(e_, 7))
                                        elif c == 7:
                                            load_chunk(("wd", e_, tg, 0, 2),
                                                       wd_ap(e_, 0, 2))
                                            load_chunk(("wd", e_, tg, 0, 3),
                                                       wd_ap(e_, 0, 3))
                                    sg_, gdg = slot_ld[("wg", e_, tg, c)]
                                    su_, gdu = slot_ld[("wu", e_, tg, c)]
                                    fo = (f % 4) * P
                                    pb_ = f % 2
                                    wts = []
                                    if f == 0:
                                        wts += [("v", xet_done),
                                                ("v", prev_evac_all)]
                                    if f % 4 == 0:
                                        wts.append((f"gd{sg_}", gdg))
                                    if silu_hist[pb_]:
                                        wts.append(("s", silu_hist[pb_]))
                                    gl = 0
                                    for k in range(KT):
                                        gl = pr.op(
                                            "tensor", "matmul",
                                            (gp[:, pb_, 0:CAP],
                                             warena[:, sg_ * 8 + k, fo:fo + P],
                                             xet[:, k, :]),
                                            dict(start=(k == 0),
                                                 stop=(k == KT - 1)),
                                            waits=(wts if k == 0 else []),
                                            inc="t")
                                    wtu = []
                                    if f == 0:
                                        wtu.append(("v", prev_evac_all))
                                    if f % 4 == 0:
                                        wtu.append((f"gd{su_}", gdu))
                                    if h_hist[pb_]:
                                        wtu.append(("v", h_hist[pb_]))
                                    ul = 0
                                    for k in range(KT):
                                        ul = pr.op(
                                            "tensor", "matmul",
                                            (up[:, pb_, 0:CAP],
                                             warena[:, su_ * 8 + k, fo:fo + P],
                                             xet[:, k, :]),
                                            dict(start=(k == 0),
                                                 stop=(k == KT - 1)),
                                            waits=(wtu if k == 0 else []),
                                            inc="t")
                                    sb = f % 2
                                    ws = [("t", gl)]
                                    if prev_h_v:
                                        ws.append(("v", prev_h_v))
                                    prev_silu_s = pr.op(
                                        "scalar", "activation",
                                        (sg[:, sb, :], gp[:, pb_, 0:CAP],
                                         AF.Silu),
                                        waits=ws, inc="s")
                                    silu_hist[pb_] = prev_silu_s
                                    prev_h_v = pr.op(
                                        "vector", "tensor_tensor",
                                        (ht[:, f, :], sg[:, sb, :],
                                         up[:, pb_, 0:CAP], ALU.mult),
                                        waits=[("s", prev_silu_s), ("t", ul)],
                                        inc="v")
                                    h_hist[pb_] = prev_h_v
                                    if f % 4 == 3:
                                        cdone = f // 4
                                        for kind in ("wg", "wu"):
                                            sl_, _ = slot_ld.pop(
                                                (kind, e_, tg, cdone))
                                            slot_busy_t[sl_] = C["t"]
                            return C["v"]

                        def emit_hd_scatter(e_, h_all_v, ps_done):
                            nonlocal prev_hds_rd
                            eb2 = e_ % 2
                            psx = psb[eb2]
                            with ExitStack() as hes:
                                hdp = hes.enter_context(
                                    nc.psum_tensor(f"hp_{tag}{tg}{e_}",
                                                   [P, NST, 512], F32))
                                spp = hes.enter_context(
                                    nc.psum_tensor(f"sp_{tag}{tg}{e_}",
                                                   [P, 2, 512], F32))
                                after_chunk = {
                                    (0, 0): [("wd", e_, tg, 1, 0)],
                                    (0, 1): [("wd", e_, tg, 1, 1)],
                                    (0, 2): [("wd", e_, tg, 1, 2)],
                                    (0, 3): [("wd", e_, tg, 1, 3)],
                                    (1, 0): ([("wg", e_ + 1, tg, 0)]
                                             if e_ + 1 < ne else []),
                                    (1, 1): ([("wu", e_ + 1, tg, 0)]
                                             if e_ + 1 < ne else []),
                                }
                                prev_sct = [0, 0]
                                hds_ev = 0
                                pending_sc = []
                                for dc in range(2):
                                    evac_t = {}
                                    for c in range(4):
                                        sd_, gdd = slot_ld[("wd", e_, tg, dc, c)]
                                        if dc == 1 and c == 1 and pending_sc:
                                            for fn in pending_sc:
                                                fn()
                                            pending_sc.clear()
                                        for st in range(NST):
                                            w = []
                                            if st == 0:
                                                w.append((f"gd{sd_}", gdd))
                                            if c == 0:
                                                w.append(("v", h_all_v))
                                                if dc == 1:
                                                    w.append(("v", hds_ev))
                                            hl = 0
                                            for fi in range(KT):
                                                hl = pr.op(
                                                    "tensor", "matmul",
                                                    (hdp[:, st, :],
                                                     ht[:, c * 8 + fi,
                                                        st * P:(st + 1) * P],
                                                     warena[:, sd_ * 8 + fi, :]),
                                                    dict(start=(c == 0 and
                                                                fi == 0),
                                                         stop=(c == 3 and
                                                               fi == KT - 1)),
                                                    waits=(w if fi == 0 else []),
                                                    inc="t")
                                            if c == 3:
                                                evac_t[st] = hl
                                        sl_, _ = slot_ld.pop(("wd", e_, tg,
                                                              dc, c))
                                        slot_busy_t[sl_] = C["t"]
                                        for key in after_chunk.get((dc, c), []):
                                            if key[0] == "wd":
                                                load_chunk(key,
                                                           wd_ap(key[1], key[3],
                                                                 key[4]))
                                            elif key[0] == "wg":
                                                load_chunk(key, wg_ap(key[1], 0))
                                            else:
                                                load_chunk(key, wu_ap(key[1], 0))
                                    w = [("t", evac_t[NST - 1])]
                                    if prev_hds_rd:
                                        w.append(("t", prev_hds_rd))
                                    hds_ev = pr.op(
                                        "vector", "tensor_copy",
                                        (hds[:, :, :], hdp[:, :, :]),
                                        waits=w, inc="v")
                                    def emit_scatter(dc=dc, hds_ev=hds_ev,
                                                     hdslo=0):
                                        nonlocal prev_hds_rd
                                        for tt in range(8):
                                            t_ = tg * 8 + tt
                                            sb2 = tt % 2
                                            sl = 0
                                            for st in range(NST):
                                                w = []
                                                if st == 0:
                                                    w = [("v", hds_ev),
                                                         ("v", ps_done),
                                                         ("v", prev_sct[sb2])]
                                                sl = pr.op(
                                                    "tensor", "matmul",
                                                    (spp[:, sb2, :],
                                                     psx[:, tt * NST + st,
                                                         hdslo:hdslo + P]
                                                     if False else
                                                     psx[:, tt * NST + st, :],
                                                     hds[:, hdslo + st, :]),
                                                    dict(start=(st == 0),
                                                         stop=(st == NST - 1)),
                                                    waits=w, inc="t")
                                            prev_sct[sb2] = pr.op(
                                                "vector",
                                                "scalar_tensor_tensor",
                                                (xs[:, t_,
                                                    dc * 512:(dc + 1) * 512],
                                                 spp[:, sb2, :],
                                                 cwb[:, t_, e_:e_ + 1],
                                                 xs[:, t_,
                                                    dc * 512:(dc + 1) * 512],
                                                 ALU.mult, ALU.add),
                                                waits=[("t", sl)], inc="v")
                                        prev_hds_rd = C["t"]
                                    if dc == 0:
                                        pending_sc.append(emit_scatter)
                                    else:
                                        emit_scatter()
                            ps_rd_t[eb2] = C["t"]
                            return max(prev_sct)

                        pg0 = emit_build_dve(0)
                        bstate[0] = (pg0, emit_ps_transposes(0, pg0))
                        for e_ in range(ne):
                            pg_done, ps_done = bstate.pop(e_)
                            xet_done = emit_gather(e_, pg_done, ps_done)
                            if e_ + 1 < ne:
                                npg = emit_build_dve(e_ + 1)
                            h_all_v = emit_gu(e_, xet_done, prev_evac_all)
                            if e_ + 1 < ne:
                                nps = emit_ps_transposes(e_ + 1, npg)
                                bstate[e_ + 1] = (npg, nps)
                            prev_evac_all = emit_hd_scatter(e_, h_all_v,
                                                            ps_done)


        def ln2_phase():
            frame_barrier()
            x_ready_v = C["v"]
            with ExitStack() as fes:
                yst = fes.enter_context(nc.sbuf_tensor("yst_l2", [P, 2, D], F32))
                ybf = fes.enter_context(nc.sbuf_tensor("ybf_l2", [P, 2, D], BF16))
                prev_dma = [0, 0]
                stats_v = {}
                ab = {}
                pssq = 0
                for t in range(NT):
                    ab[t] = ln_stats_a(t, yst[:, 0, :], x_ready_v, pssq)
                    pssq = ab[t][1]
                for t in range(NT):
                    stats_v[t] = ln_stats_b(t, *ab[t])
                for t in range(NT):
                    buf = t % 2
                    v8, _ = ln_norm(t, stats_v[t], yst[:, buf, :],
                                    ybf[:, buf, :], None,
                                    extra_v8=[(f"ldc{buf}", prev_dma[buf])])
                    prev_dma[buf] = pr.op(
                        "sync", "dma_start",
                        kwargs=dict(out=cc_in3[:, t, :], in_=ybf[:, buf, :]),
                        waits=[("v", v8)], inc=f"ldc{buf}", incby=16)

        def attention():
            ln2_phase()
            pr.op("gpsimd", "collective_compute",
                  ("AllGather", ALU.bypass, [[0, 1], [2, 3], [4, 5], [6, 7]]),
                  dict(ins=[cc_in.ap().opt()], outs=[cc_out.ap().opt()]),
                  waits=[("ldc0", C["ldc0"]), ("ldc1", C["ldc1"])], inc="cc")

            frame_barrier()
            with ExitStack() as aes:
                y2f = aes.enter_context(nc.sbuf_tensor("y2f", [P, NGT, D], BF16))
                y2hhT = [aes.enter_context(
                    nc.sbuf_tensor(f"y2hhT{i}", [P, 4096], BF16))
                    for i in range(2)]
                yhK = [aes.enter_context(nc.sbuf_tensor(f"yhK{i}", [P, 4096],
                                                        BF16)) for i in range(2)]
                yhQ = [aes.enter_context(nc.sbuf_tensor(f"yhQ{i}", [P, 4096],
                                                        BF16)) for i in range(2)]
                va = [aes.enter_context(nc.sbuf_tensor(f"vA{i}", [P, NGT, 66],
                                                       BF16)) for i in range(2)]
                ysq = aes.enter_context(nc.sbuf_tensor("ysq", [P, 1024], BF16))
                masks_s = aes.enter_context(
                    nc.sbuf_tensor("masks_s", [P, 8, 512], BF16))
                pT = [aes.enter_context(nc.sbuf_tensor(f"pT{i}", [P, 2, 512],
                                                       BF16)) for i in range(3)]
                otsb = [aes.enter_context(nc.sbuf_tensor(f"ots{i}", [P, 512],
                                                         F32)) for i in range(2)]
                rcps = aes.enter_context(nc.sbuf_tensor("rcps", [P, 4], F32))

                pr.op("sync", "dma_start",
                      kwargs=dict(out=masks_s[:, :, :], in_=masks_h[:, :, :]),
                      inc="ldm", incby=16)
                for gb in range(8):
                    start = (gb % 2) * 2048 + (gb // 2) * 512
                    src = cc_out.ap()[start:start + 512, :].rearrange(
                        "(t p) d -> p t d", p=P)
                    pr.op("sync", "dma_start",
                          kwargs=dict(out=y2f[:, gb * 4:(gb + 1) * 4, :],
                                      in_=src),
                          waits=[("cc", 1)], inc="ldy", incby=16)
                y2_ld = C["ldy"]

                pair_rd_t = [0, 0]
                yh_rd_t = [0, 0]
                va_rd_t = [0, 0]
                ot_evac_v = 0
                tr2_rd_v = 0
                ysq_rd_t = 0
                bq_done_prev = 0
                pending = []
                s_bar = 0
                v_bar = 0
                NLOOK = 2
                NSB = 3

                for hp2 in range(8):
                    pb = hp2 % 2
                    with ExitStack() as tfr:
                        trps = [tfr.enter_context(
                            nc.psum_tensor(f"atrp{hp2}{i}", [P, 4, P], BF16))
                            for i in range(2)]
                        tr_busy = [0, 0]
                        cps = []
                        for g4 in range(8):
                            bank = g4 % 2
                            tc_ = 0
                            for j in range(4):
                                gt = g4 * 4 + j
                                w = [("v", v_idbf), ("ldy", y2_ld)]
                                if j == 0:
                                    w.append(("v", tr_busy[bank]))
                                if gt == 0:
                                    w += [("t", pair_rd_t[pb]),
                                          ("s", s_bar), ("v", v_bar)]
                                tc_ = pr.op("tensor", "transpose",
                                            (trps[bank][:, j, :],
                                             y2f[:, gt, hp2 * P:(hp2 + 1) * P],
                                             idbf[:, :]),
                                            waits=w, inc="t")
                            vc = pr.op("vector", "tensor_copy",
                                       (y2hhT[pb][:, g4 * 512:(g4 + 1) * 512],
                                        trps[bank][:, :, :]),
                                       waits=[("t", tc_)], inc="v")
                            tr_busy[bank] = vc
                            cps.append(vc)
                        pair_v = max(cps)

                    with ExitStack() as pes:
                        spps = [pes.enter_context(
                            nc.psum_tensor(f"aspp{hp2}{i}", [P, 2, 512], F32))
                            for i in range(NSB)]
                        otps = pes.enter_context(
                            nc.psum_tensor(f"aotp{hp2}", [P, 512], F32))
                        trp2 = pes.enter_context(
                            nc.psum_tensor(f"atr2{hp2}", [P, P], F32))
                        sp_cons = [("s", 0)] * NSB

                        for hi in range(2):
                            h = hp2 * 2 + hi
                            yb = h % 2
                            wprev = [("t", yh_rd_t[yb])]
                            vK = pr.op("vector", "tensor_copy",
                                       (yhK[yb][0:HD, :],
                                        y2hhT[pb][hi * HD:(hi + 1) * HD, :]),
                                       waits=[("v", pair_v)] + wprev, inc="v")
                            vK2 = pr.op("vector", "memset",
                                        (yhK[yb][HD:HD + 1, :], -1.0),
                                        waits=wprev, inc="v")
                            vQ = pr.op("vector", "tensor_copy",
                                       (yhQ[yb][0:HD, :],
                                        y2hhT[pb][hi * HD:(hi + 1) * HD, :]),
                                       waits=[("v", pair_v)] + wprev, inc="v")
                            bq_chunk = []
                            bq_last = [bq_done_prev, bq_done_prev]
                            spq_used = 0
                            for hq in range(4):
                                sqs = pr.op("scalar", "activation",
                                            (ysq[:, :],
                                             y2hhT[pb][:, hq * 1024:
                                                       (hq + 1) * 1024],
                                             AF.Square),
                                            waits=[("v", pair_v),
                                                   ("t", ysq_rd_t)], inc="s")
                                for ch in range(2):
                                    gci = hq * 2 + ch
                                    par = gci % 2
                                    q0 = hq * 1024 + ch * 512
                                    tgt = (otps[0:1, :] if par == 0
                                           else spps[0][0:1, 0, :])
                                    w = [("s", sqs), ("v", v_onec),
                                         ("s", bq_last[par])]
                                    if par == 0:
                                        w.append(("v", ot_evac_v))
                                    else:
                                        w.append(sp_cons[0])
                                    bt = pr.op(
                                        "tensor", "matmul",
                                        (tgt,
                                         onecol[hi * HD:(hi + 1) * HD, :],
                                         ysq[hi * HD:(hi + 1) * HD,
                                             ch * 512:(ch + 1) * 512]),
                                        dict(start=True, stop=True),
                                        waits=w, inc="t")
                                    bq_last[par] = pr.op(
                                        "scalar", "activation",
                                        (yhQ[yb][HD:HD + 1, q0:q0 + 512],
                                         tgt, AF.Sqrt),
                                        dict(scale=1024.0),
                                        waits=[("t", bt), ("v", vQ)], inc="s")
                                    bq_chunk.append(bq_last[par])
                                    if par == 1:
                                        spq_used = bq_last[par]
                                ysq_rd_t = C["t"]
                            bq_done_prev = max(bq_last)
                            bq_sp_guard = spq_used
                            vb_ = h % 2
                            wva = [("ldy", y2_ld), ("t", va_rd_t[vb_])]
                            va1 = pr.op("vector", "tensor_copy",
                                        (va[vb_][:, :, 0:HD],
                                         y2f[:, :, h * HD:(h + 1) * HD]),
                                        waits=wva, inc="v")
                            va2 = pr.op("vector", "memset",
                                        (va[vb_][:, :, HD:HD + 1], 1.0),
                                        waits=wva, inc="v")
                            va_done = max(va1, va2)

                            for s_ in range(SLOTS):
                                nkt = 8 * s_ + 8
                                G = nkt // 2
                                bq_gate = bq_chunk[min(2 * s_ + 1, 7)]

                                def emit_qk(g, s_=s_, yb=yb, bq_gate=bq_gate,
                                            vK=vK, vK2=vK2):
                                    sb = g % NSB
                                    qk = 0
                                    for j in range(2):
                                        kt = 2 * g + j
                                        wqk = [("v", vK), ("v", vK2),
                                               ("s", bq_gate)]
                                        if g == 0 and s_ == 0 and j == 0:
                                            wqk.append(("s", bq_sp_guard))
                                        if j == 0:
                                            wqk.append(sp_cons[sb])
                                        qk = pr.op(
                                            "tensor", "matmul",
                                            (spps[sb][:, j, :],
                                             yhK[yb][0:HD + 1,
                                                     kt * P:(kt + 1) * P],
                                             Dyn(lambda e, yb=yb, s_=s_:
                                                 yhQ[yb][0:HD + 1, bass.ds(
                                                     qoff_expr(e, s_), 512)])),
                                            dict(start=True, stop=True),
                                            waits=wqk, inc="t")
                                    return qk

                                qk_cnt = {}
                                for g0 in range(min(NLOOK, G)):
                                    qk_cnt[g0] = emit_qk(g0)
                                    if g0 == 0:
                                        for fn in pending:
                                            fn()
                                        pending.clear()
                                pv_last = 0
                                for g in range(G):
                                    sb = g % NSB
                                    ex = pr.op("scalar", "activation",
                                               (pT[sb][:, :, :],
                                                spps[sb][:, :, :],
                                                AF.Exp), dict(scale=0.125),
                                               waits=[("t", qk_cnt[g])],
                                               inc="s")
                                    sp_cons[sb] = ("s", ex)
                                    pt_ready = [("s", ex), ("s", ex)]
                                    for j in range(2):
                                        kt = 2 * g + j
                                        if kt >= nkt - 8:
                                            jj = kt - (nkt - 8)
                                            mv = pr.op(
                                                "vector", "tensor_tensor",
                                                (pT[sb][:, j, :],
                                                 pT[sb][:, j, :],
                                                 masks_s[:, jj, :], ALU.mult),
                                                waits=[("s", ex), ("ldm", 16)],
                                                inc="v")
                                            sp_cons[sb] = ("v", mv)
                                            pt_ready[j] = ("v", mv)
                                    if g + NLOOK < G:
                                        qk_cnt[g + NLOOK] = emit_qk(g + NLOOK)
                                    for j in range(2):
                                        kt = 2 * g + j
                                        wpv = [pt_ready[j], ("v", va_done)]
                                        if kt == 0:
                                            wpv.append(("v", ot_evac_v))
                                        pv_last = pr.op(
                                            "tensor", "matmul",
                                            (otps[0:HD + 1, :],
                                             va[vb_][:, kt, 0:HD + 1],
                                             pT[sb][:, j, :]),
                                            dict(start=(kt == 0),
                                                 stop=(kt == nkt - 1)),
                                            waits=wpv, inc="t")
                                ob = s_ % 2
                                ev = pr.op("vector", "tensor_copy",
                                           (otsb[ob][0:HD + 1, :],
                                            otps[0:HD + 1, :]),
                                           waits=[("t", pv_last)], inc="v")
                                ot_evac_v = ev

                                def deferred(ev=ev, ob=ob, s_=s_, h=h,
                                             trp2=trp2):
                                    nonlocal tr2_rd_v
                                    for q4 in range(4):
                                        t2 = pr.op(
                                            "tensor", "transpose",
                                            (trp2[0:P, 0:HD + 1],
                                             otsb[ob][0:HD + 1,
                                                      q4 * P:(q4 + 1) * P],
                                             id32[0:HD + 1, 0:HD + 1]),
                                            waits=[("v", ev), ("v", tr2_rd_v)],
                                            inc="t")
                                        rc = pr.op("vector", "reciprocal",
                                                   (rcps[:, q4:q4 + 1],
                                                    trp2[:, HD:HD + 1]),
                                                   waits=[("t", t2)], inc="v")
                                        tr2_rd_v = pr.op(
                                            "vector", "scalar_tensor_tensor",
                                            (xs[:, s_ * 4 + q4,
                                                h * HD:(h + 1) * HD],
                                             trp2[:, 0:HD], rcps[:, q4:q4 + 1],
                                             xs[:, s_ * 4 + q4,
                                                h * HD:(h + 1) * HD],
                                             ALU.mult, ALU.add),
                                            waits=[("v", rc)], inc="v")
                                pending.append(deferred)
                            yh_rd_t[yb] = C["t"]
                            va_rd_t[vb_] = C["t"]
                        for fn in pending:
                            fn()
                        pending.clear()
                        pair_rd_t[pb] = C["t"]
                        s_bar = C["s"]
                        v_bar = C["v"]

        if CFG["DO_MOE1"]:
            moe_layer("m1")
        if CFG["DO_ATT"]:
            attention()
        if CFG["DO_MOE3"]:
            moe_layer("m3")

        pr.op("sync", "dma_start", kwargs=dict(out=x_out3, in_=xs[:, :, :]),
              waits=[("v", C["v"]), ("s", C["s"]), ("t", C["t"])],
              inc="st", incby=16)
        stf = C["st"]

        block = es.enter_context(nc.Block())

        @block.sync
        def _(sync):
            pr.emit(sync, "sync", sems)
            sync.wait_ge(sems["st"], stf)

        @block.gpsimd
        def _(gpsimd):
            pr.emit(gpsimd, "gpsimd", sems)

        @block.vector
        def _(vector):
            pr.emit(vector, "vector", sems)

        @block.scalar
        def _(scalar):
            pr.emit(scalar, "scalar", sems)

        @block.tensor
        def _(tensor):
            pr.emit(tensor, "tensor", sems)

    return nc


def make_masks(parity):
    k = np.arange(P)[:, None]
    q = np.arange(512)[None, :]
    m = np.zeros((P, 8, 512), dtype=ml_dtypes.bfloat16)
    for j in range(8):
        if parity == 0:
            if j < 4:
                m[:, j, :] = (q >= 128 * j + k).astype(ml_dtypes.bfloat16)
        else:
            if j < 4:
                m[:, j, :] = 1
            else:
                m[:, j, :] = (q >= 128 * (j - 4) + k).astype(ml_dtypes.bfloat16)
    return m


def kernel(L0_attn_x, L0_ffn_router_w, L0_ffn_w_gate, L0_ffn_w_up, L0_ffn_w_down):
    x = np.ascontiguousarray(np.asarray(L0_attn_x, dtype=np.float32))
    rw = np.ascontiguousarray(np.asarray(L0_ffn_router_w, dtype=np.float32))
    wg = np.ascontiguousarray(np.asarray(L0_ffn_w_gate, dtype=np.float32))
    wu = np.ascontiguousarray(np.asarray(L0_ffn_w_up, dtype=np.float32))
    wd = np.ascontiguousarray(np.asarray(L0_ffn_w_down, dtype=np.float32))
    ident = np.eye(P, dtype=np.float32)
    tril = (np.arange(P)[:, None] < np.arange(P)[None, :]).astype(
        ml_dtypes.bfloat16)
    iota = np.broadcast_to(np.arange(384, dtype=np.float16), (P, 384)).copy()

    nc = build_nc()
    in_maps = []
    for c in range(8):
        b, p = c // 2, c % 2
        shard = np.concatenate(
            [x[b, (2 * s + p) * 512:(2 * s + p + 1) * 512, :] for s in range(4)],
            axis=0)
        in_maps.append({
            "x_in": np.ascontiguousarray(shard),
            "rw": rw, "wg": wg, "wu": wu, "wd": wd,
            "masks": make_masks(p), "ident32": ident,
            "trilput": tril, "iota384": iota,
        })
    res = run_bass_kernel_spmd(nc, in_maps, core_ids=list(range(8)))
    out = np.empty((4, 4096, D), dtype=np.float32)
    for c in range(8):
        b, p = c // 2, c % 2
        shard = np.asarray(res.results[c]["x_outp"])
        for s in range(4):
            out[b, (2 * s + p) * 512:(2 * s + p + 1) * 512, :] = \
                shard[s * 512:(s + 1) * 512, :]
    return out
